# version 04
# baseline (speedup 1.0000x reference)
import numpy as np
import concourse.bass as bass
import concourse.mybir as mybir
from concourse.bass_utils import run_bass_kernel_spmd

F32 = mybir.dt.float32
BF16 = mybir.dt.bfloat16
AF = mybir.ActivationFunctionType
ALU = mybir.AluOpType
AX = mybir.AxisListType

P = 128
T = 4096
TOWN = 2048
D = 1024
INNER = 4096
NT = T // P
NOWN = TOWN // P
CD = D // P
NKC = T // P
RC = 12582912.0
DEBUG = False

ENGINES = ("sync", "gpsimd", "tensor", "vector", "scalar")


class Sem:
    __slots__ = ("name", "n", "handle")

    def __init__(self, name):
        self.name, self.n, self.handle = name, 0, None


class Bld:
    def __init__(self):
        self.streams = {e: [] for e in ENGINES}
        self.sems = []

    def sem(self, name):
        s = Sem(name)
        self.sems.append(s)
        return s

    def emit(self, eng, fn, waits=(), inc=None, step=1, drain=None):
        ws = tuple((s, v) for (s, v) in waits if v > 0)
        if drain is None:
            drain = eng in ("vector", "scalar")
        if drain:
            self.streams[eng].append((fn, ws, None, 1))
            self.streams[eng].append((lambda e: e.drain(), (), inc, step))
        else:
            self.streams[eng].append((fn, ws, inc, step))
        if inc is not None:
            inc.n += step
            return inc.n
        return None

    def dma(self, eng, out, in_, waits=(), inc=None, **kw):
        return self.emit(eng, lambda e: e.dma_start(out=out, in_=in_, **kw),
                         waits, inc, 16)


def build_core():
    nc = bass.Bass("TRN2", target_bir_lowering=False)
    bd = Bld()
    sSY = bd.sem("sSY")
    sGP = bd.sem("sGP")
    sGC = bd.sem("sGC")
    sTE = bd.sem("sTE")
    sVE = bd.sem("sVE")
    sSC = bd.sem("sSC")
    E = bd.emit
    DMA = bd.dma

    x_h = nc.dram_tensor("x", [T, D], F32, kind="ExternalInput").ap()
    wqT_h = nc.dram_tensor("wqT", [D, D], F32, kind="ExternalInput").ap()
    wkT_h = nc.dram_tensor("wkT", [D, D], F32, kind="ExternalInput").ap()
    wvT_h = nc.dram_tensor("wvT", [D, D], F32, kind="ExternalInput").ap()
    bq_h = nc.dram_tensor("bq", [D], F32, kind="ExternalInput").ap()
    bk_h = nc.dram_tensor("bk", [D], F32, kind="ExternalInput").ap()
    bv_h = nc.dram_tensor("bv", [D], F32, kind="ExternalInput").ap()
    wgT_h = nc.dram_tensor("wgT", [D, INNER], F32, kind="ExternalInput").ap()
    wuT_h = nc.dram_tensor("wuT", [D, INNER], F32, kind="ExternalInput").ap()
    wdT_h = nc.dram_tensor("wdT", [INNER, D], F32, kind="ExternalInput").ap()
    out_h = nc.dram_tensor("out", [TOWN, D], F32, kind="ExternalOutput").ap()

    kd = "ExternalOutput" if DEBUG else "Internal"
    xn_h = nc.dram_tensor("xn_d", [T, D], BF16, kind=kd).ap()
    xi_h = nc.dram_tensor("xi_d", [TOWN, D], BF16, kind=kd).ap()
    v_h = nc.dram_tensor("v_d", [T, D], BF16, kind=kd).ap()
    q_h = nc.dram_tensor("q_d", [D, TOWN], BF16, kind=kd).ap()
    at_h = nc.dram_tensor("at_d", [TOWN, D], BF16, kind=kd).ap()
    hi_h = nc.dram_tensor("hi_d", [TOWN, INNER], BF16, kind=kd).ap()
    if DEBUG:
        k_dbg = nc.dram_tensor("k_dbg", [D, T], BF16, kind="ExternalOutput").ap()
        s_dbg = nc.dram_tensor("s_dbg", [P, 9 * NT], F32, kind="ExternalOutput").ap()

    AR_K = nc.alloc_sbuf_tensor("AR_K", [P, 8 * 4096], BF16)
    AR_PT = nc.alloc_sbuf_tensor("AR_PT", [P, 8 * 4096], BF16)
    AR_W = nc.alloc_sbuf_tensor("AR_W", [P, 8 * 3072], BF16)
    MSC8 = nc.alloc_sbuf_tensor("MSC8", [P, 4096], BF16)
    STR8 = nc.alloc_sbuf_tensor("STR8", [P, 4096], BF16)
    P1B = nc.alloc_sbuf_tensor("P1B", [P, 4096], BF16)

    KF = AR_K.reshape([P, CD, T])
    WG = AR_K.reshape([P, CD, INNER])
    HFM = AR_K.reshape([P, NKC, 1024])
    A32 = AR_PT.bitcast(F32)
    XNQ = AR_PT.reshape([P, 32, 1024])
    PT = AR_PT.reshape([P, NKC, 1024])
    WU = AR_PT.reshape([P, CD, INNER])
    WD = AR_PT.reshape([P, NKC, 1024])
    WQKV = AR_W.reshape([P, 3, CD, 1024])
    AWB = AR_W
    AWF = AR_W.bitcast(F32)
    V48 = AR_W.reshape([P, 48, 512])
    Q24 = AR_W.reshape([P, 24, 1024])
    XI12 = AR_W.reshape([P, 12, 2048])
    M8F = MSC8.bitcast(F32)
    S8F = STR8.bitcast(F32)
    P8F = P1B.bitcast(F32)

    def sm(name, shape, dt=F32):
        return nc.alloc_sbuf_tensor(name, shape, dt)

    s1_all = sm("s1_all", [P, NT])
    s2_all = sm("s2_all", [P, NT])
    m_all = sm("m_all", [P, NT])
    vp_all = sm("vp_all", [P, NT])
    t1_all = sm("t1_all", [P, NT])
    svp_all = sm("svp_all", [P, NT])
    r_all = sm("r_all", [P, NT])
    bact_all = sm("bact_all", [P, NT])
    ax_all = sm("ax_all", [P, NOWN])
    t_all = sm("t_all", [P, NOWN])
    sx_all = sm("sx_all", [P, NOWN])
    isx_all = sm("isx_all", [P, NOWN])
    gscale_all = sm("gscale_all", [P, NOWN])
    thr_all = sm("thr_all", [P, NOWN])
    dscale_all = sm("dscale_all", [P, NOWN])
    bk_cols = sm("bk_cols", [P, CD])
    bq_cols = sm("bq_cols", [P, CD])
    bq32_cols = sm("bq32_cols", [P, CD])
    bv_bf = sm("bv_bf", [1, D], BF16)
    onesr = sm("onesr", [1, P], BF16)
    negr = sm("negr", [1, P], BF16)
    onesc = sm("onesc", [P, 1], BF16)
    onescf = sm("onescf", [P, 1], F32)
    ones_rf = sm("ones_rf", [1, P], F32)
    ones11f = sm("ones11f", [1, 1], F32)
    c127 = sm("c127", [P, 1], F32)
    kmax11 = sm("kmax11", [1, 1], F32)
    tmax11 = sm("tmax11", [1, 1], F32)
    u_col = sm("u_col", [P, 1], F32)
    i0_col = sm("i0_col", [P, 1], F32)
    e1_col = sm("e1_col", [P, 1], F32)
    mk2 = sm("mk2", [1, 1], F32)
    wsum = sm("wsum", [P, 3], F32)
    part_col = sm("part_col", [P, 1], F32)
    cg_col = sm("cg_col", [P, 3], F32)
    sg_col = sm("sg_col", [P, 3], F32)
    cc_col = sm("cc_col", [P, 1], F32)
    linv_cols = sm("linv_cols", [P, 8], F32)
    hm_col = sm("hm_col", [P, 1], F32)
    tc_col = sm("tc_col", [P, 1], F32)
    hs_col = sm("hs_col", [P, 1], F32)
    d1_col = sm("d1_col", [P, 1], F32)
    cgv = sm("cgv", [1, 3], F32)

    PS = [nc.alloc_psum_tensor(f"ps{i}", [P, 1024], F32) for i in range(4)]

    E("vector", lambda e: e.memset(onesr[:], 1.0))
    E("vector", lambda e: e.memset(negr[:], -1.0))
    E("vector", lambda e: e.memset(onesc[:], 1.0))
    E("vector", lambda e: e.memset(onescf[:], 1.0))
    E("vector", lambda e: e.memset(ones_rf[:], 1.0))
    E("vector", lambda e: e.memset(ones11f[:], 1.0))
    E("vector", lambda e: e.memset(c127[:], 127.0))
    consts_done = E("vector", lambda e: e.memset(wsum[:], 0.0), inc=sVE)

    for wi, wT in ((0, wqT_h), (1, wkT_h), (2, wvT_h)):
        DMA("gpsimd", WQKV[:, wi, :, :],
            wT.rearrange("(c p) d -> p c d", p=P), inc=sGP)
    wqkv_done = sGP.n
    DMA("gpsimd", bk_cols[:], bk_h.rearrange("(c p) -> p c", p=P), inc=sGP,
        allow_slow_non_contiguous=True)
    DMA("gpsimd", bq_cols[:], bq_h.rearrange("(c p) -> p c", p=P), inc=sGP,
        allow_slow_non_contiguous=True)
    DMA("gpsimd", bv_bf[:], bv_h.rearrange("(o d) -> o d", o=1), inc=sGP)
    bias_done = sGP.n
    bq32_done = E("vector", lambda e: e.tensor_scalar(
        bq32_cols[:], bq_cols[:], 1.0 / 32.0, None, ALU.mult),
        waits=[(sGP, bias_done)], inc=sVE)

    strip_srcs = []
    for c in range(CD):
        for s in range(4):
            strip_srcs.append(wgT_h[c * P:(c + 1) * P, s * 1024:(s + 1) * 1024])
    for c in range(CD):
        for s in range(4):
            strip_srcs.append(wuT_h[c * P:(c + 1) * P, s * 1024:(s + 1) * 1024])
    for ic in range(NKC):
        strip_srcs.append(wdT_h[ic * P:(ic + 1) * P, :])
    NS = len(strip_srcs)
    p1_bufs = [P8F[:, 0:1024], P8F[:, 1024:2048]]
    p1_dma_cnt = [None] * NS
    p1_vec_cnt = [None] * NS
    p1 = {"dma": 0, "vec": 0}

    def p1_emit_dma():
        si = p1["dma"]
        if si >= NS:
            return
        w = []
        if si >= 2 and p1_vec_cnt[si - 2] is not None:
            w.append((sVE, p1_vec_cnt[si - 2]))
        p1_dma_cnt[si] = DMA("gpsimd", p1_bufs[si % 2], strip_srcs[si],
                             waits=w, inc=sGP)
        p1["dma"] += 1

    def p1_emit_vec():
        si = p1["vec"]
        if si >= NS or p1_dma_cnt[si] is None:
            return
        wcol = si // 32
        E("vector", lambda e, b=p1_bufs[si % 2]: e.tensor_reduce(
            part_col[:], b, AX.X, ALU.add, apply_absolute_value=True),
          waits=[(sGP, p1_dma_cnt[si])])
        p1_vec_cnt[si] = E("vector", lambda e, w=wcol: e.tensor_add(
            wsum[:, w:w + 1], wsum[:, w:w + 1], part_col[:]),
            waits=[(sVE, consts_done)], inc=sVE)
        p1["vec"] += 1

    p1_emit_dma()
    p1_emit_dma()

    xA = [A32[:, 8192:9216], A32[:, 9216:10240]]
    xnf = [A32[:, 10240:11264], A32[:, 11264:12288]]
    xsq = [A32[:, 12288:13312], A32[:, 13312:14336]]
    vppad = A32[:, 14336:14848]
    lnpadA = A32[:, 14848:15360]
    rpadA = A32[:, 15360:15872]
    xib2 = [AR_PT[:, 24576:25600], AR_PT[:, 25600:26624]]
    xnb = [MSC8[:, 0:1024], MSC8[:, 1024:2048]]
    xib = [MSC8[:, 2048:3072], MSC8[:, 3072:4096]]

    cons = {0: [], 1: []}
    xsq_free = {}
    for i in range(NT):
        par = i % 2
        ld = DMA("sync", xA[par], x_h[i * P:(i + 1) * P, :],
                 waits=cons[par], inc=sSY)
        wsq = [(sSY, ld)]
        if par in xsq_free:
            wsq.append(xsq_free[par])
        sc = E("scalar", lambda e, p=par: e.activation(
            xsq[p], xA[p], AF.Square), waits=wsq, inc=sSC, drain=False)
        vc = E("vector", lambda e, p=par, i=i: e.reduce_sum(
            s1_all[:, i:i + 1], xA[p], AX.X), waits=[(sSY, ld)], inc=sVE)
        vc2s = E("vector", lambda e, p=par, i=i: e.reduce_sum(
            s2_all[:, i:i + 1], xsq[p], AX.X), waits=[(sSC, sc)], inc=sVE)
        xsq_free[par] = (sVE, vc2s)
        cons[par] = [(sSC, sc), (sVE, vc)]
    a1_sc = sSC.n

    E("vector", lambda e: e.tensor_scalar(
        m_all[:], s1_all[:], 1.0 / D, None, ALU.mult))
    E("vector", lambda e: e.tensor_scalar(
        vp_all[:], s2_all[:], 1.0 / D, None, ALU.mult))
    E("vector", lambda e: e.tensor_copy(svp_all[:], m_all[:]))
    E("vector", lambda e: e.tensor_mul(t1_all[:], m_all[:], svp_all[:]))
    E("vector", lambda e: e.tensor_sub(vp_all[:], vp_all[:], t1_all[:]))
    E("vector", lambda e: e.tensor_scalar(
        vp_all[:], vp_all[:], 1e-5, None, ALU.add))
    E("vector", lambda e: e.memset(vppad[:], 1.0))
    a2v1 = E("vector", lambda e: e.tensor_copy(vppad[:, 0:NT], vp_all[:]),
             inc=sVE)
    E("scalar", lambda e: e.activation(lnpadA[:], vppad[:], AF.Ln),
      waits=[(sVE, a2v1)])
    a2s = E("scalar", lambda e: e.activation(rpadA[:], lnpadA[:], AF.Exp,
                                             scale=-0.5), inc=sSC)
    E("vector", lambda e: e.tensor_copy(r_all[:], rpadA[:, 0:NT]),
      waits=[(sSC, a2s)])
    E("vector", lambda e: e.tensor_mul(t1_all[:], vp_all[:], r_all[:]))
    E("vector", lambda e: e.tensor_mul(t1_all[:], t1_all[:], r_all[:]))
    E("vector", lambda e: e.tensor_scalar(
        t1_all[:], t1_all[:], -0.5, 1.5, ALU.mult, ALU.add))
    E("vector", lambda e: e.tensor_mul(r_all[:], r_all[:], t1_all[:]))
    E("vector", lambda e: e.tensor_mul(t1_all[:], m_all[:], r_all[:]))
    a2_done = E("vector", lambda e: e.tensor_scalar(
        bact_all[:], t1_all[:], -1.0, None, ALU.mult), inc=sVE)
    if DEBUG:
        DMA("sync", s_dbg[:, 7 * NT:8 * NT], r_all[:],
            waits=[(sVE, a2_done)], inc=sSY)
        DMA("gpsimd", s_dbg[:, 0:NT], s1_all[:], waits=[(sVE, a2_done)], inc=sGP)
        DMA("gpsimd", s_dbg[:, NT:2 * NT], s2_all[:], inc=sGP)
        DMA("gpsimd", s_dbg[:, 2 * NT:3 * NT], r_all[:], inc=sGP)
        DMA("gpsimd", s_dbg[:, 3 * NT:4 * NT], bact_all[:], inc=sGP)
        DMA("gpsimd", s_dbg[:, 4 * NT:5 * NT], vp_all[:], inc=sGP)
        DMA("gpsimd", s_dbg[:, 5 * NT:6 * NT], svp_all[:], inc=sGP)

    xnb_free = {}
    xn_store_cnt = {}
    for i in range(NT):
        par = i % 2
        ld = DMA("sync", xA[par], x_h[i * P:(i + 1) * P, :],
                 waits=cons[par], inc=sSY)
        w2 = [(sSY, ld), (sVE, a2_done)]
        if par in xnb_free:
            w2.append(xnb_free[par])
        sc1 = E("scalar", lambda e, p=par, i=i: e.activation(
            xnb[p], xA[p], AF.Identity, bias=bact_all[:, i:i + 1],
            scale=r_all[:, i:i + 1]), waits=w2, inc=sSC, drain=False)
        DMA("sync", xn_h[i * P:(i + 1) * P, :], xnb[par],
            waits=[(sSC, sc1)], inc=sSY)
        xn_store_cnt[i] = sSY.n
        xnb_free[par] = (sSY, sSY.n)
        cons[par] = [(sSC, sc1)]
    a3_stores = sSY.n

    vt_bf = [MSC8[:, 0:1024], MSC8[:, 1024:2048]]
    qt_bf = [MSC8[:, 2048:3072], MSC8[:, 3072:4096]]
    evacK = {0: 0, 1: 0}
    evacV = {0: 0, 1: 0}
    vt_free = {0: 0, 1: 0}
    qt_free = {0: 0, 1: 0}
    xnq_mm_done = {0: 0, 1: 0}

    for qr in range(4):
        qp = qr % 2
        tq = None
        for c in range(CD):
            w = []
            if c == 0:
                w = [(sSY, xn_store_cnt[qr * 8 + 7]),
                     (sTE, xnq_mm_done[qp]),
                     (sSC, sSC.n)]
            tq = DMA("sync", XNQ[:, qp * 8 + c, :],
                     xn_h[qr * 1024:(qr + 1) * 1024, c * P:(c + 1) * P],
                     waits=w, inc=sSY, transpose=True)
        xnq_done = tq

        for o in range(CD):
            par = o % 2
            ps = PS[par]
            for c in range(CD):
                w = []
                if c == 0:
                    w = [(sSY, xnq_done), (sGP, wqkv_done), (sSC, evacK[par])]
                lhsT = WQKV[:, 1, c, o * P:(o + 1) * P]
                E("tensor", lambda e, l=lhsT, p=ps, c=c, qp=qp:
                  e.matmul(p[:, 0:512], l, XNQ[:, qp * 8 + c, 0:512],
                           start=(c == 0), stop=(c == CD - 1),
                           skip_group_check=True), waits=w)
                E("tensor", lambda e, l=lhsT, p=ps, c=c, qp=qp:
                  e.matmul(p[:, 512:1024], l, XNQ[:, qp * 8 + c, 512:1024],
                           start=(c == 0), stop=(c == CD - 1),
                           skip_group_check=True),
                  inc=(sTE if c == CD - 1 else None))
            grp = sTE.n
            evacK[par] = E("scalar", lambda e, p=ps, o=o, qr=qr: e.activation(
                KF[:, o, qr * 1024:(qr + 1) * 1024], p[:], AF.Identity,
                bias=bk_cols[:, o:o + 1]),
                waits=[(sTE, grp), (sGP, bias_done)], inc=sSC, drain=False)

        for tb in range(CD):
            par = tb % 2
            ps = PS[2 + par]
            for c in range(CD):
                w = []
                if c == 0:
                    w = [(sSY, xnq_done), (sGP, bias_done), (sSC, evacV[par]),
                         (sVE, consts_done)]
                lhsT = XNQ[:, qp * 8 + c, tb * P:(tb + 1) * P]
                E("tensor", lambda e, l=lhsT, p=ps, c=c:
                  e.matmul(p[:, 0:512], l, WQKV[:, 2, c, 0:512],
                           start=(c == 0), stop=False,
                           skip_group_check=True), waits=w)
                E("tensor", lambda e, l=lhsT, p=ps, c=c:
                  e.matmul(p[:, 512:1024], l, WQKV[:, 2, c, 512:1024],
                           start=(c == 0), stop=False,
                           skip_group_check=True))
            E("tensor", lambda e, p=ps: e.matmul(
                p[:, 0:512], onesr[0:1, :], bv_bf[0:1, 0:512],
                start=False, stop=True, skip_group_check=True))
            E("tensor", lambda e, p=ps: e.matmul(
                p[:, 512:1024], onesr[0:1, :], bv_bf[0:1, 512:1024],
                start=False, stop=True, skip_group_check=True), inc=sTE)
            grp = sTE.n
            evacV[par] = E("scalar", lambda e, p=ps, par=par: e.copy(
                vt_bf[par], p[:]), waits=[(sTE, grp), (sSY, vt_free[par])],
                inc=sSC, drain=False)
            vt_free[par] = DMA(
                "sync", v_h[qr * 1024 + tb * P: qr * 1024 + (tb + 1) * P, :],
                vt_bf[par], waits=[(sSC, evacV[par])], inc=sSY)

        if qr < 2:
            for o in range(CD):
                par = o % 2
                ps = PS[par]
                for c in range(CD):
                    w = []
                    if c == 0:
                        w = [(sSY, xnq_done), (sSC, evacK[par]),
                             (sVE, bq32_done)]
                    lhsT = WQKV[:, 0, c, o * P:(o + 1) * P]
                    E("tensor", lambda e, l=lhsT, p=ps, c=c, qp=qp:
                      e.matmul(p[:, 0:512], l, XNQ[:, qp * 8 + c, 0:512],
                               start=(c == 0), stop=(c == CD - 1),
                               skip_group_check=True), waits=w)
                    E("tensor", lambda e, l=lhsT, p=ps, c=c, qp=qp:
                      e.matmul(p[:, 512:1024], l,
                               XNQ[:, qp * 8 + c, 512:1024],
                               start=(c == 0), stop=(c == CD - 1),
                               skip_group_check=True),
                      inc=(sTE if c == CD - 1 else None))
                grp = sTE.n
                evacK[par] = E("scalar", lambda e, p=ps, o=o, par=par:
                               e.activation(qt_bf[par], p[:], AF.Identity,
                                            bias=bq32_cols[:, o:o + 1],
                                            scale=1.0 / 32.0),
                               waits=[(sTE, grp), (sSY, qt_free[par])],
                               inc=sSC, drain=False)
                qt_free[par] = DMA(
                    "sync",
                    q_h[o * P:(o + 1) * P, qr * 1024:(qr + 1) * 1024],
                    qt_bf[par], waits=[(sSC, evacK[par])], inc=sSY)
        xnq_mm_done[qp] = sTE.n

        for _ in range(24):
            p1_emit_vec()
            p1_emit_dma()
    while p1["vec"] < NS:
        p1_emit_vec()
        p1_emit_dma()
    p1_done = sVE.n
    b_te = sTE.n
    b_sc = sSC.n
    b_sy = sSY.n

    xq_free = {}
    xib2_free = {}
    a3b_gp = sGP.n
    for i in range(NOWN):
        par = i % 2
        wld = list(xq_free.get(par, []))
        if i < 2:
            wld += cons[par]
        ldq = DMA("gpsimd", xA[par], x_h[i * P:(i + 1) * P, :],
                  waits=wld, inc=sGP)
        wsq2 = [(sGP, ldq), (sVE, a2_done)] + list(xq_free.get(par, []))
        scq = E("scalar", lambda e, p=par, i=i: e.activation(
            xnf[p], xA[p], AF.Identity, bias=bact_all[:, i:i + 1],
            scale=r_all[:, i:i + 1]), waits=wsq2,
            inc=sSC, drain=False)
        E("vector", lambda e, p=par, i=i: e.reduce_max(
            ax_all[:, i:i + 1], xnf[p], AX.X, apply_absolute_value=True),
          waits=[(sSC, scq)])
        E("vector", lambda e, i=i: e.tensor_scalar(
            t_all[:, i:i + 1], ax_all[:, i:i + 1], 1e-5, None, ALU.max))
        E("vector", lambda e, i=i: e.tensor_scalar(
            isx_all[:, i:i + 1], t_all[:, i:i + 1], 1.0 / 127.0,
            None, ALU.mult))
        vq = E("vector", lambda e, i=i: e.tensor_copy(
            vppad[:, 0:1], t_all[:, i:i + 1]), inc=sVE)
        E("scalar", lambda e: e.activation(lnpadA[:], vppad[:], AF.Ln),
          waits=[(sVE, vq)])
        scx = E("scalar", lambda e: e.activation(
            rpadA[:], lnpadA[:], AF.Exp, scale=-1.0), inc=sSC)
        E("vector", lambda e, i=i: e.tensor_tensor(
            e1_col[:], t_all[:, i:i + 1], rpadA[:, 0:1], ALU.mult),
          waits=[(sSC, scx)])
        E("vector", lambda e: e.tensor_scalar(
            e1_col[:], e1_col[:], -1.0, 2.0, ALU.mult, ALU.add))
        E("vector", lambda e: e.tensor_tensor(
            e1_col[:], rpadA[:, 0:1], e1_col[:], ALU.mult))
        sxv = E("vector", lambda e, i=i: e.tensor_scalar(
            sx_all[:, i:i + 1], e1_col[:], 127.0, None, ALU.mult), inc=sVE)
        E("vector", lambda e, p=par, i=i: e.tensor_tensor(
            xnf[p], xnf[p], sx_all[:, i:i + 1].broadcast_to((P, 1024)),
            ALU.mult), waits=[(sVE, sxv)], drain=False)
        wib = [(sSY, xib2_free[par])] if par in xib2_free else []
        xiv = E("vector", lambda e, p=par: e.tensor_scalar(
            xib2[p], xnf[p], RC, RC, ALU.add, ALU.subtract),
            waits=wib, inc=sVE, drain=False)
        xq_free[par] = [(sVE, xiv)]
        DMA("sync", xi_h[i * P:(i + 1) * P, :], xib2[par],
            waits=[(sVE, xiv)], inc=sSY)
        xib2_free[par] = sSY.n
    a3b_done = sVE.n
    a3b_gp = sGP.n
    a3b_sy = sSY.n

    if DEBUG:
        for c in range(CD):
            DMA("gpsimd", k_dbg[c * P:(c + 1) * P, :], KF[:, c, :],
                waits=[(sSC, b_sc)], inc=sGP)

    accb = AWB[:, 0:4096]
    tmpb = AWB[:, 4096:8192]
    knr = AWF[:, 4096:8192]
    ks0 = E("scalar", lambda e: e.activation(accb, KF[:, 0, :], AF.Square),
            waits=[(sSC, b_sc), (sTE, b_te)], inc=sSC, drain=False)
    kn_v = None
    for c in range(1, CD):
        ksc = E("scalar", lambda e, c=c: e.activation(
            tmpb, KF[:, c, :], AF.Square),
            waits=([(sVE, kn_v)] if kn_v else []), inc=sSC, drain=False)
        kn_v = E("vector", lambda e: e.tensor_add(accb, accb, tmpb),
                 waits=[(sSC, ksc)], inc=sVE)
    km_init = E("vector", lambda e: e.memset(kmax11[:], 0.0), inc=sVE)
    kn_prev = km_init
    for j in range(8):
        mmj = E("tensor", lambda e, j=j: e.matmul(
            PS[0][0:1, 0:512], onesc[:, 0:1], accb[:, j * 512:(j + 1) * 512],
            start=True, stop=True, skip_group_check=True),
            waits=[(sVE, kn_v if j == 0 else kn_prev),
                   (sSC, b_sc)], inc=sTE)
        E("vector", lambda e: e.reduce_max(
            tmax11[:], PS[0][0:1, 0:512], AX.X), waits=[(sTE, mmj)])
        kn_prev = E("vector", lambda e: e.tensor_max(
            kmax11[:], kmax11[:], tmax11[:]), inc=sVE)
    mk2_done = E("vector", lambda e: e.tensor_scalar(
        mk2[:], kmax11[:], 1.01, None, ALU.mult), inc=sVE)

    c_row = M8F[0:1, 0:1024]
    c_bf = MSC8[0:1, 2048:3072]
    at_bf = [MSC8[:, 3072:3584], MSC8[:, 3584:4096]]
    u_row = S8F[0:1, 0:1024]
    u2_row = P8F[0:1, 0:1024]
    yq_row = P8F[0:1, 1024:2048]
    linv_row = S8F[0:1, 1024:2048]
    accq = AWB[:, 0:1024]
    tmpq = AWB[:, 1024:2048]

    pv_te = b_te
    at_free = {0: 0, 1: 0}
    exp_cnt = {}
    linvc_prev = 0
    for hq in range(2):
        qld = None
        for c in range(CD):
            w = []
            if c == 0:
                w = [(sSY, b_sy), (sTE, pv_te), (sSC, sSC.n)]
            qld = DMA("sync", Q24[:, 16 + c, :],
                      q_h[c * P:(c + 1) * P, hq * 1024:(hq + 1) * 1024],
                      waits=w, inc=sSY)
        E("scalar", lambda e: e.activation(
            accq, Q24[:, 16, :], AF.Square), waits=[(sSY, qld)],
          drain=False)
        qn_v = None
        for c in range(1, CD):
            qsc = E("scalar", lambda e, c=c: e.activation(
                tmpq, Q24[:, 16 + c, :], AF.Square),
                waits=([(sVE, qn_v)] if qn_v else []), inc=sSC, drain=False)
            qn_v = E("vector", lambda e: e.tensor_add(accq, accq, tmpq),
                     waits=[(sSC, qsc)], inc=sVE)
        qm1 = E("tensor", lambda e: e.matmul(
            PS[0][0:1, 0:512], onesc[:, 0:1], accq[:, 0:512],
            start=True, stop=True, skip_group_check=True),
            waits=[(sVE, qn_v)], inc=sTE)
        qm2 = E("tensor", lambda e: e.matmul(
            PS[0][0:1, 512:1024], onesc[:, 0:1], accq[:, 512:1024],
            start=True, stop=True, skip_group_check=True), inc=sTE)
        yq = E("vector", lambda e: e.tensor_scalar(
            yq_row, PS[0][0:1, 0:1024], mk2[0:1, 0:1], None, ALU.mult),
            waits=[(sTE, qm2), (sVE, mk2_done)], inc=sVE)
        lnq = E("scalar", lambda e: e.activation(
            u_row, yq_row, AF.Ln), waits=[(sVE, yq)], inc=sSC)
        uhop = E("vector", lambda e: e.tensor_copy(u2_row, u_row),
                 waits=[(sSC, lnq)], inc=sVE)
        c_s = E("scalar", lambda e: e.activation(
            c_row, u2_row, AF.Exp, scale=0.5), waits=[(sVE, uhop)], inc=sSC)
        cbf_done = E("vector", lambda e: e.tensor_copy(c_bf, c_row),
                     waits=[(sSC, c_s)], inc=sVE)

        for kc in range(NKC):
            par = kc % 2
            ps = PS[par]
            for c in range(CD):
                w = []
                if c == 0:
                    w = [(sVE, cbf_done),
                         (sSC, exp_cnt.get((hq, kc - 2),
                                           exp_cnt.get((hq - 1, NKC - 2 + kc),
                                                       0)))]
                    if hq == 0 and kc == 16:
                        w += [(sVE, a3b_done), (sSY, a3b_sy)]
                lhsT = KF[:, c, kc * P:(kc + 1) * P]
                E("tensor", lambda e, l=lhsT, p=ps, c=c:
                  e.matmul(p[:, 0:512], l, Q24[:, 16 + c, 0:512],
                           start=(c == 0), stop=False,
                           skip_group_check=True), waits=w)
                E("tensor", lambda e, l=lhsT, p=ps, c=c:
                  e.matmul(p[:, 512:1024], l, Q24[:, 16 + c, 512:1024],
                           start=(c == 0), stop=False,
                           skip_group_check=True))
            E("tensor", lambda e, p=ps: e.matmul(
                p[:, 0:512], negr[0:1, :], c_bf[0:1, 0:512],
                start=False, stop=True, skip_group_check=True))
            E("tensor", lambda e, p=ps: e.matmul(
                p[:, 512:1024], negr[0:1, :], c_bf[0:1, 512:1024],
                start=False, stop=True, skip_group_check=True), inc=sTE)
            grp = sTE.n
            exp_cnt[(hq, kc)] = E("scalar", lambda e, p=ps, kc=kc:
                                  e.activation(PT[:, kc, :], p[:], AF.Exp),
                                  waits=[(sTE, grp)], inc=sSC, drain=False)
            wl = [(sSC, exp_cnt[(hq, kc)])]
            if kc == 0:
                wl.append((sVE, linvc_prev))
            E("tensor", lambda e, kc=kc: e.matmul(
                PS[2][0:1, 0:512], onesc[:, 0:1], PT[:, kc, 0:512],
                start=(kc == 0), stop=(kc == NKC - 1),
                skip_group_check=True), waits=wl)
            E("tensor", lambda e, kc=kc: e.matmul(
                PS[2][0:1, 512:1024], onesc[:, 0:1], PT[:, kc, 512:1024],
                start=(kc == 0), stop=(kc == NKC - 1),
                skip_group_check=True),
              inc=(sTE if kc == NKC - 1 else None))
        l_done = sTE.n
        scores_te_mark = l_done
        lnl = E("scalar", lambda e: e.activation(
            u_row, PS[2][0:1, 0:1024], AF.Ln),
            waits=[(sTE, l_done)], inc=sSC)
        lhop = E("vector", lambda e: e.tensor_copy(u2_row, u_row),
                 waits=[(sSC, lnl)], inc=sVE)
        lv = E("scalar", lambda e: e.activation(
            linv_row, u2_row, AF.Exp, scale=-1.0),
            waits=[(sVE, lhop)], inc=sSC)
        for qt in range(8):
            E("tensor", lambda e, qt=qt: e.matmul(
                PS[2][:, qt:qt + 1], linv_row[0:1, qt * P:(qt + 1) * P],
                ones11f[0:1, 0:1], start=True, stop=True,
                skip_group_check=True),
              waits=([(sSC, lv)] if qt == 0 else []),
              inc=(sTE if qt == 7 else None))
        linvc = E("vector", lambda e: e.tensor_copy(
            linv_cols[:], PS[2][:, 0:8]), waits=[(sTE, sTE.n)], inc=sVE)
        linvc_prev = linvc

        for dh in range(2):
            vld = None
            varea_guard = sTE.n
            for g in range(8):
                w = []
                if g == 0:
                    w = [(sTE, varea_guard), (sSY, b_sy)]
                vld = DMA(
                    "sync", V48[:, g * 4:(g + 1) * 4, :],
                    v_h.rearrange("(k p) d -> p k d", p=P)[
                        :, g * 4:(g + 1) * 4, dh * 512:(dh + 1) * 512],
                    waits=w, inc=sSY)
            for qt in range(8):
                par = qt % 2
                psO = PS[3][:, par * 512:(par + 1) * 512]
                for kc in range(NKC):
                    w = []
                    if kc == 0:
                        w = [(sSY, vld), (sVE, linvc),
                             (sSC, at_free.get(par, 0))]
                    E("tensor", lambda e, p=psO, kc=kc, qt=qt:
                      e.matmul(p, PT[:, kc, qt * P:(qt + 1) * P],
                               V48[:, kc, :], start=(kc == 0),
                               stop=(kc == NKC - 1),
                               skip_group_check=True),
                      waits=w, inc=(sTE if kc == NKC - 1 else None))
                grp = sTE.n
                ev = E("scalar", lambda e, p=psO, par=par, qt=qt:
                       e.activation(at_bf[par], p, AF.Identity,
                                    scale=linv_cols[:, qt:qt + 1]),
                       waits=[(sTE, grp)], inc=sSC, drain=False)
                at_free[par] = ev
                row0 = hq * 1024 + qt * P
                DMA("sync", at_h[row0:row0 + P, dh * 512:(dh + 1) * 512],
                    at_bf[par], waits=[(sSC, ev)], inc=sSY)
        if hq == 1:
            scores_te = scores_te_mark
        pv_te = sTE.n
    d_sy = sSY.n
    d_te = sTE.n

    d_ve = sVE.n
    d_gc = sGC.n
    d_sc = sSC.n
    cg_done = {}
    cv_prev = 0
    csc_prev = 0
    for w in range(3):
        mmn = E("tensor", lambda e, w=w: e.matmul(
            PS[0][0:1, 0:1], wsum[:, w:w + 1], onescf[:, 0:1],
            start=True, stop=True, skip_group_check=True),
            waits=([(sVE, p1_done), (sTE, scores_te),
                    (sSC, exp_cnt[(1, NKC - 1)])]
                   if w == 0 else [(sVE, cv_prev)]), inc=sTE)
        cv = E("vector", lambda e, w=w: e.tensor_scalar(
            cgv[0:1, w:w + 1], PS[0][0:1, 0:1],
            1.0 / (D * INNER), 1e-5, ALU.mult, ALU.max),
            waits=[(sTE, mmn)], inc=sVE)
        cv_prev = cv
        mmb = E("tensor", lambda e, w=w: e.matmul(
            PS[1][:, 0:1], ones_rf[0:1, :], cgv[0:1, w:w + 1],
            start=True, stop=True, skip_group_check=True),
            waits=[(sVE, cv), (sVE, csc_prev)], inc=sTE)
        csc = E("vector", lambda e, w=w: e.tensor_copy(
            cg_col[:, w:w + 1], PS[1][:, 0:1]), waits=[(sTE, mmb)])
        csc2 = E("vector", lambda e, w=w: e.tensor_copy(
            vppad[:, 0:1], cg_col[:, w:w + 1]), inc=sVE)
        csc_prev = csc2
        E("scalar", lambda e: e.activation(
            lnpadA[:], vppad[:], AF.Ln), waits=[(sVE, csc2)])
        sgx = E("scalar", lambda e: e.activation(
            rpadA[:], lnpadA[:], AF.Exp, scale=-1.0), inc=sSC)
        E("vector", lambda e, w=w: e.tensor_tensor(
            e1_col[:], cg_col[:, w:w + 1], rpadA[:, 0:1], ALU.mult),
          waits=[(sSC, sgx)])
        E("vector", lambda e: e.tensor_scalar(
            e1_col[:], e1_col[:], -1.0, 2.0, ALU.mult, ALU.add))
        cg_done[w] = E("vector", lambda e, w=w: e.tensor_tensor(
            sg_col[:, w:w + 1], rpadA[:, 0:1], e1_col[:], ALU.mult), inc=sVE)
    E("vector", lambda e: e.tensor_tensor(
        gscale_all[:], isx_all[:], cg_col[:, 0:1].broadcast_to((P, NOWN)),
        ALU.mult), waits=[(sVE, cg_done[2])])
    E("vector", lambda e: e.tensor_tensor(
        thr_all[:], sx_all[:], sg_col[:, 1:2].broadcast_to((P, NOWN)),
        ALU.mult))
    E("vector", lambda e: e.tensor_scalar(
        thr_all[:], thr_all[:], 1e-5, None, ALU.mult))
    E("vector", lambda e: e.tensor_tensor(
        cc_col[:], cg_col[:, 1:2], cg_col[:, 2:3], ALU.mult))
    eprol = E("vector", lambda e: e.tensor_scalar(
        cc_col[:], cc_col[:], 1.0 / 127.0, None, ALU.mult), inc=sVE)
    eprol_sc = sSC.n

    q1 = [P8F[:, 0:1024], P8F[:, 1024:2048]]
    q1_free = {0: 0, 1: 0}
    wq_ready = {}
    ea0_i = 0
    for w, dst, wsrc, area_te in ((0, WG, wgT_h, scores_te),
                                  (1, WU, wuT_h, d_te)):
        for s in range(4):
            for c in range(CD):
                par = ea0_i % 2
                ea0_i += 1
                wld = [(sVE, q1_free[par])]
                if ea0_i <= 2:
                    wld.append((sVE, p1_done))
                ld = DMA("gpsimd", q1[par],
                         wsrc[c * P:(c + 1) * P, s * 1024:(s + 1) * 1024],
                         waits=wld, inc=sGP)
                E("vector", lambda e, par=par, w=w: e.tensor_tensor(
                    q1[par], q1[par],
                    sg_col[:, w:w + 1].broadcast_to((P, 1024)), ALU.mult),
                    waits=[(sGP, ld), (sVE, cg_done[w]), (sTE, area_te)],
                    drain=False)
                E("vector", lambda e, par=par: e.tensor_scalar(
                    q1[par], q1[par], RC, RC, ALU.add, ALU.subtract),
                  drain=False)
                q1_free[par] = E(
                    "vector", lambda e, par=par, d=dst, c=c, s=s:
                    e.tensor_scalar(d[:, c, s * 1024:(s + 1) * 1024],
                                    q1[par], 1.0, -1.0, ALU.min, ALU.max),
                    inc=sVE)
                wq_ready[(w, c, s)] = q1_free[par]
    ea0_ve = sVE.n

    xld = None
    for c in range(CD):
        w = []
        if c == 0:
            w = [(sTE, d_te), (sSY, d_sy), (sVE, d_ve), (sGC, d_gc)]
        xld = DMA("sync", XI12[:, c, 0:2048],
                  xi_h[0:TOWN, c * P:(c + 1) * P], waits=w, inc=sSY,
                  transpose=True)
    H = AWF[:, 8192:12288]
    gsil = S8F[:, 0:1024]
    hib = MSC8[:]
    gu_ev = {}
    hi_st_last = 0
    hi_tile_gp = {}
    dsc_done = {}
    for t in range(NOWN):
        for iq in range(4):
            gi = t * 4 + iq
            pg = PS[0] if iq % 2 == 0 else PS[2]
            pu = PS[1] if iq % 2 == 0 else PS[3]
            for c in range(CD):
                w = []
                if c == 0:
                    w = [(sSY, xld)]
                    if gi == 0:
                        w.append((sSC, eprol_sc))
                    pe = gu_ev.get(gi - 2)
                    if pe:
                        w += [(sSC, pe[0]), (sVE, pe[1])]
                    w.append((sVE, wq_ready[(1, CD - 1, iq)]))
                lhsT = XI12[:, c, t * P:(t + 1) * P]
                E("tensor", lambda e, l=lhsT, p=pg, c=c, iq=iq:
                  e.matmul(p[:, 0:512], l,
                           WG[:, c, iq * 1024:iq * 1024 + 512],
                           start=(c == 0), stop=(c == CD - 1),
                           skip_group_check=True), waits=w)
                E("tensor", lambda e, l=lhsT, p=pg, c=c, iq=iq:
                  e.matmul(p[:, 512:1024], l,
                           WG[:, c, iq * 1024 + 512:(iq + 1) * 1024],
                           start=(c == 0), stop=(c == CD - 1),
                           skip_group_check=True))
                E("tensor", lambda e, l=lhsT, p=pu, c=c, iq=iq:
                  e.matmul(p[:, 0:512], l,
                           WU[:, c, iq * 1024:iq * 1024 + 512],
                           start=(c == 0), stop=(c == CD - 1),
                           skip_group_check=True))
                E("tensor", lambda e, l=lhsT, p=pu, c=c, iq=iq:
                  e.matmul(p[:, 512:1024], l,
                           WU[:, c, iq * 1024 + 512:(iq + 1) * 1024],
                           start=(c == 0), stop=(c == CD - 1),
                           skip_group_check=True),
                  inc=(sTE if c == CD - 1 else None))
            grp = sTE.n
            wsil = [(sTE, grp), (sVE, eprol)]
            if gi >= 1:
                wsil.append((sVE, gu_ev[gi - 1][1]))
            g_sc = E("scalar", lambda e, p=pg, t=t: e.activation(
                gsil, p[:], AF.Silu, scale=gscale_all[:, t:t + 1]),
                waits=wsil, inc=sSC, drain=False)
            h_vc = E("vector", lambda e, p=pu, iq=iq: e.tensor_tensor(
                H[:, iq * 1024:(iq + 1) * 1024], gsil, p[:], ALU.mult),
                waits=[(sSC, g_sc)], inc=sVE, drain=False)
            gu_ev[gi] = (g_sc, h_vc)
        E("vector", lambda e: e.reduce_max(
            hm_col[:], H, AX.X, apply_absolute_value=True))
        E("vector", lambda e, t=t: e.tensor_tensor(
            tc_col[:], hm_col[:], thr_all[:, t:t + 1], ALU.max))
        tcv = E("vector", lambda e: e.tensor_copy(
            gsil[:, 0:1], tc_col[:]), inc=sVE)
        E("scalar", lambda e: e.activation(
            gsil[:, 512:1024], gsil[:, 0:512], AF.Ln), waits=[(sVE, tcv)])
        hsx = E("scalar", lambda e: e.activation(
            gsil[:, 0:512], gsil[:, 512:1024], AF.Exp, scale=-1.0), inc=sSC)
        E("vector", lambda e: e.tensor_tensor(
            e1_col[:], tc_col[:], gsil[:, 0:1], ALU.mult), waits=[(sSC, hsx)])
        E("vector", lambda e: e.tensor_scalar(
            e1_col[:], e1_col[:], -1.0, 2.0, ALU.mult, ALU.add))
        E("vector", lambda e: e.tensor_tensor(
            e1_col[:], gsil[:, 0:1], e1_col[:], ALU.mult))
        E("vector", lambda e: e.tensor_scalar(
            hs_col[:], e1_col[:], 127.0, None, ALU.mult))
        E("vector", lambda e, t=t: e.tensor_tensor(
            d1_col[:], tc_col[:], isx_all[:, t:t + 1], ALU.mult))
        dsc_done[t] = E("vector", lambda e, t=t: e.tensor_tensor(
            dscale_all[:, t:t + 1], d1_col[:], cc_col[:], ALU.mult), inc=sVE)
        for s4 in range(4):
            w = [(sGP, hi_st_last)] if s4 == 0 else []
            if s4 == 0 and t == 0:
                w.append((sSY, d_sy))
            E("vector", lambda e, s4=s4: e.tensor_tensor(
                H[:, s4 * 1024:(s4 + 1) * 1024],
                H[:, s4 * 1024:(s4 + 1) * 1024],
                hs_col[:].broadcast_to((P, 1024)), ALU.mult), waits=w,
              drain=False)
            E("vector", lambda e, s4=s4: e.tensor_scalar(
                hib[:, s4 * 1024:(s4 + 1) * 1024],
                H[:, s4 * 1024:(s4 + 1) * 1024], RC, RC,
                ALU.add, ALU.subtract),
              inc=(sVE if s4 == 3 else None), drain=(s4 == 3))
        hv = sVE.n
        hi_st_last = DMA("gpsimd", hi_h[t * P:(t + 1) * P, :], hib[:],
                         waits=[(sVE, hv)], inc=sGP)
        hi_tile_gp[t] = hi_st_last
    ea1_te = sTE.n
    ea1_gp = sGP.n
    ea1_sc = sSC.n
    ea1_ve = sVE.n

    atb = [AWB[:, 0:1024], AWB[:, 1024:2048]]
    xb = [AWF[:, 1024:2048], AWF[:, 2048:3072]]
    otb = [AWF[:, 3072:4096], AWF[:, 4096:5120]]
    oub = [AWF[:, 5120:6144], AWF[:, 6144:7168]]
    wdb = [P8F[:, 0:1024], P8F[:, 1024:2048]]
    wd_ready = {}
    q2ebb = S8F[:, 0:1024]
    wdb_free = {0: 0, 1: 0}
    ld_free = {0: [], 1: []}
    ot_ev = {0: 0, 1: 0}
    ou_done = {0: 0, 1: 0}
    down_te_half = ea1_te
    out_total = 0
    for th in range(2):
        hld = None
        for ic in range(NKC):
            w = []
            if ic == 0:
                w = [(sGP, hi_tile_gp[th * 8 + 7]), (sTE, down_te_half)]
            hld = DMA("sync", HFM[:, ic, :],
                      hi_h[th * 1024:(th + 1) * 1024, ic * P:(ic + 1) * P],
                      waits=w, inc=sSY, transpose=True)
        if th == 0:
            for ic in range(NKC):
                par = ic % 2
                wld = [(sVE, wdb_free[par])]
                if ic < 2:
                    wld.append((sVE, ea0_ve))
                ld = DMA("gpsimd", wdb[par],
                         wdT_h[ic * P:(ic + 1) * P, :],
                         waits=wld, inc=sGP)
                E("vector", lambda e, par=par: e.tensor_tensor(
                    wdb[par], wdb[par],
                    sg_col[:, 2:3].broadcast_to((P, 1024)), ALU.mult),
                  waits=[(sGP, ld)], drain=False)
                E("vector", lambda e, par=par: e.tensor_scalar(
                    wdb[par], wdb[par], RC, RC, ALU.add, ALU.subtract),
                  drain=False)
                wdb_free[par] = E(
                    "vector", lambda e, par=par, ic=ic: e.tensor_scalar(
                        WD[:, ic, :], wdb[par], 1.0, -1.0, ALU.min, ALU.max),
                    inc=sVE)
                wd_ready[ic] = wdb_free[par]
        for qt in range(8):
            tile = th * 8 + qt
            par = qt % 2
            psD = PS[par]
            ald = DMA("gpsimd", atb[par],
                      at_h[tile * P:(tile + 1) * P, :],
                      waits=list(ld_free[par]) + [(sSY, d_sy)], inc=sGP)
            xldb = DMA("gpsimd", xb[par], x_h[tile * P:(tile + 1) * P, :],
                       waits=ld_free[par], inc=sGP)
            for ic in range(NKC):
                w = []
                if ic == 0:
                    w = [(sSY, hld), (sSC, ot_ev[par])]
                    if th == 0 and qt < 2:
                        w += [(sSC, ea1_sc), (sVE, ea1_ve)]
                if th == 0:
                    w.append((sVE, wd_ready[ic]))
                lhsT = HFM[:, ic, qt * P:(qt + 1) * P]
                E("tensor", lambda e, l=lhsT, p=psD, ic=ic:
                  e.matmul(p[:, 0:512], l, WD[:, ic, 0:512],
                           start=(ic == 0), stop=(ic == NKC - 1),
                           skip_group_check=True), waits=w)
                E("tensor", lambda e, l=lhsT, p=psD, ic=ic:
                  e.matmul(p[:, 512:1024], l, WD[:, ic, 512:1024],
                           start=(ic == 0), stop=(ic == NKC - 1),
                           skip_group_check=True),
                  inc=(sTE if ic == NKC - 1 else None))
            grp = sTE.n
            ot_ev[par] = E("scalar", lambda e, p=psD, par=par, tile=tile:
                           e.activation(otb[par], p[:], AF.Identity,
                                        scale=dscale_all[:, tile:tile + 1]),
                           waits=[(sTE, grp), (sVE, dsc_done[tile])],
                           inc=sSC, drain=False)
            E("vector", lambda e, par=par: e.tensor_add(
                otb[par], otb[par], atb[par]),
              waits=[(sSC, ot_ev[par]), (sGP, ald), (sVE, ou_done[par])],
              drain=False)
            ov = E("vector", lambda e, par=par: e.tensor_add(
                oub[par], otb[par], xb[par]),
                waits=[(sGP, xldb)], inc=sVE, drain=False)
            ou_done[par] = ov
            ost = DMA("sync", out_h[tile * P:(tile + 1) * P, :], oub[par],
                      waits=[(sVE, ov)], inc=sSY)
            ld_free[par] = [(sVE, ov)]
        down_te_half = sTE.n
    out_total = sSY.n

    E("sync", lambda e: e.nop(), waits=[(sSY, out_total)])
    E("gpsimd", lambda e: e.nop(), waits=[(sGP, sGP.n)])

    for s in bd.sems:
        s.handle = nc.alloc_semaphore(name=s.name)
    with nc.Block() as block:
        def mk(name):
            def run(eng):
                for fn, waits, inc, step in bd.streams[name]:
                    for (s, v) in waits:
                        eng.wait_ge(s.handle, v)
                    ins = fn(eng)
                    if inc is not None:
                        ins.then_inc(inc.handle, step)
            return run
        block.sync(mk("sync"))
        block.gpsimd(mk("gpsimd"))
        block.tensor(mk("tensor"))
        block.vector(mk("vector"))
        block.scalar(mk("scalar"))
    return nc


_NC_CACHE = None


def kernel(L0_attn_x, L0_attn_Wq, L0_attn_bq, L0_attn_Wk, L0_attn_bk,
           L0_attn_Wv, L0_attn_bv, L0_ffn_w_gate, L0_ffn_w_up,
           L0_ffn_w_down):
    global _NC_CACHE
    x = np.ascontiguousarray(np.asarray(L0_attn_x, dtype=np.float32))
    wq = np.asarray(L0_attn_Wq, dtype=np.float32)
    wk = np.asarray(L0_attn_Wk, dtype=np.float32)
    wv = np.asarray(L0_attn_Wv, dtype=np.float32)
    wg = np.asarray(L0_ffn_w_gate, dtype=np.float32)
    wu = np.asarray(L0_ffn_w_up, dtype=np.float32)
    wd = np.asarray(L0_ffn_w_down, dtype=np.float32)
    shared = {
        "wqT": np.ascontiguousarray(wq.T),
        "wkT": np.ascontiguousarray(wk.T),
        "wvT": np.ascontiguousarray(wv.T),
        "bq": np.ascontiguousarray(np.asarray(L0_attn_bq, dtype=np.float32)),
        "bk": np.ascontiguousarray(np.asarray(L0_attn_bk, dtype=np.float32)),
        "bv": np.ascontiguousarray(np.asarray(L0_attn_bv, dtype=np.float32)),
        "wgT": np.ascontiguousarray(wg.T),
        "wuT": np.ascontiguousarray(wu.T),
        "wdT": np.ascontiguousarray(wd.T),
    }
    in_maps = []
    for core in range(8):
        b, h = core // 2, core % 2
        xb = x[b]
        xcore = np.concatenate([xb[h * TOWN:(h + 1) * TOWN],
                                xb[(1 - h) * TOWN:(2 - h) * TOWN]], axis=0)
        m = {"x": np.ascontiguousarray(xcore)}
        m.update(shared)
        in_maps.append(m)

    if _NC_CACHE is None:
        _NC_CACHE = build_core()
    nc = _NC_CACHE
    res = run_bass_kernel_spmd(nc, in_maps, core_ids=list(range(8)))
    out = np.empty((4, T, D), dtype=np.float32)
    for core in range(8):
        b, h = core // 2, core % 2
        out[b, h * TOWN:(h + 1) * TOWN] = res.results[core]["out"]
    return out


# version 08
# speedup vs baseline: 1.0458x; 1.0458x over previous
import numpy as np
import concourse.bass as bass
import concourse.mybir as mybir
from concourse.bass_utils import run_bass_kernel_spmd

F32 = mybir.dt.float32
BF16 = mybir.dt.bfloat16
AF = mybir.ActivationFunctionType
ALU = mybir.AluOpType
AX = mybir.AxisListType

P = 128
T = 4096
TOWN = 2048
D = 1024
INNER = 4096
NT = T // P
NOWN = TOWN // P
CD = D // P
NKC = T // P
RC = 12582912.0
DEBUG = False

ENGINES = ("sync", "gpsimd", "tensor", "vector", "scalar")


class Sem:
    __slots__ = ("name", "n", "handle")

    def __init__(self, name):
        self.name, self.n, self.handle = name, 0, None


class Bld:
    def __init__(self):
        self.streams = {e: [] for e in ENGINES}
        self.sems = []

    def sem(self, name):
        s = Sem(name)
        self.sems.append(s)
        return s

    def emit(self, eng, fn, waits=(), inc=None, step=1, drain=None):
        ws = tuple((s, v) for (s, v) in waits if v > 0)
        if drain is None:
            drain = eng in ("vector", "scalar")
        if drain:
            self.streams[eng].append((fn, ws, None, 1))
            self.streams[eng].append((lambda e: e.drain(), (), inc, step))
        else:
            self.streams[eng].append((fn, ws, inc, step))
        if inc is not None:
            inc.n += step
            return inc.n
        return None

    def dma(self, eng, out, in_, waits=(), inc=None, **kw):
        return self.emit(eng, lambda e: e.dma_start(out=out, in_=in_, **kw),
                         waits, inc, 16)


def build_core():
    nc = bass.Bass("TRN2", target_bir_lowering=False)
    bd = Bld()
    sSY = bd.sem("sSY")
    sGP = bd.sem("sGP")
    sGC = bd.sem("sGC")
    sTE = bd.sem("sTE")
    sVE = bd.sem("sVE")
    sSC = bd.sem("sSC")
    E = bd.emit
    DMA = bd.dma

    x_h = nc.dram_tensor("x", [T, D], F32, kind="ExternalInput").ap()
    wqT_h = nc.dram_tensor("wqT", [D, D], F32, kind="ExternalInput").ap()
    wkT_h = nc.dram_tensor("wkT", [D, D], F32, kind="ExternalInput").ap()
    wvT_h = nc.dram_tensor("wvT", [D, D], F32, kind="ExternalInput").ap()
    bq_h = nc.dram_tensor("bq", [D], F32, kind="ExternalInput").ap()
    bk_h = nc.dram_tensor("bk", [D], F32, kind="ExternalInput").ap()
    bv_h = nc.dram_tensor("bv", [D], F32, kind="ExternalInput").ap()
    wgT_h = nc.dram_tensor("wgT", [D, INNER], F32, kind="ExternalInput").ap()
    wuT_h = nc.dram_tensor("wuT", [D, INNER], F32, kind="ExternalInput").ap()
    wdT_h = nc.dram_tensor("wdT", [INNER, D], F32, kind="ExternalInput").ap()
    out_h = nc.dram_tensor("out", [TOWN, D], F32, kind="ExternalOutput").ap()

    kd = "ExternalOutput" if DEBUG else "Internal"
    xn_h = nc.dram_tensor("xn_d", [T, D], BF16, kind=kd).ap()
    xi_h = nc.dram_tensor("xi_d", [TOWN, D], BF16, kind=kd).ap()
    v_h = nc.dram_tensor("v_d", [T, D], BF16, kind=kd).ap()
    q_h = nc.dram_tensor("q_d", [D, TOWN], BF16, kind=kd).ap()
    at_h = nc.dram_tensor("at_d", [TOWN, D], BF16, kind=kd).ap()
    hi_h = nc.dram_tensor("hi_d", [TOWN, INNER], BF16, kind=kd).ap()
    if DEBUG:
        k_dbg = nc.dram_tensor("k_dbg", [D, T], BF16, kind="ExternalOutput").ap()
        s_dbg = nc.dram_tensor("s_dbg", [P, 9 * NT], F32, kind="ExternalOutput").ap()

    AR_K = nc.alloc_sbuf_tensor("AR_K", [P, 8 * 4096], BF16)
    AR_PT = nc.alloc_sbuf_tensor("AR_PT", [P, 8 * 4096], BF16)
    AR_W = nc.alloc_sbuf_tensor("AR_W", [P, 8 * 3072], BF16)
    MSC8 = nc.alloc_sbuf_tensor("MSC8", [P, 4096], BF16)
    STR8 = nc.alloc_sbuf_tensor("STR8", [P, 4096], BF16)
    P1B = nc.alloc_sbuf_tensor("P1B", [P, 4096], BF16)

    KF = AR_K.reshape([P, CD, T])
    WG = AR_K.reshape([P, CD, INNER])
    HFM = AR_K.reshape([P, NKC, 1024])
    A32 = AR_PT.bitcast(F32)
    XNQ = AR_PT.reshape([P, 32, 1024])
    PT = AR_PT.reshape([P, NKC, 1024])
    WU = AR_PT.reshape([P, CD, INNER])
    WD = AR_PT.reshape([P, NKC, 1024])
    WQKV = AR_W.reshape([P, 3, CD, 1024])
    AWB = AR_W
    AWF = AR_W.bitcast(F32)
    V48 = AR_W.reshape([P, 48, 512])
    Q24 = AR_W.reshape([P, 24, 1024])
    XI12 = AR_W.reshape([P, 12, 2048])
    M8F = MSC8.bitcast(F32)
    S8F = STR8.bitcast(F32)
    P8F = P1B.bitcast(F32)

    def sm(name, shape, dt=F32):
        return nc.alloc_sbuf_tensor(name, shape, dt)

    s1_all = sm("s1_all", [P, NT])
    s2_all = sm("s2_all", [P, NT])
    m_all = sm("m_all", [P, NT])
    vp_all = sm("vp_all", [P, NT])
    t1_all = sm("t1_all", [P, NT])
    svp_all = sm("svp_all", [P, NT])
    r_all = sm("r_all", [P, NT])
    bact_all = sm("bact_all", [P, NT])
    ax_all = sm("ax_all", [P, NOWN])
    t_all = sm("t_all", [P, NOWN])
    sx_all = sm("sx_all", [P, NOWN])
    isx_all = sm("isx_all", [P, NOWN])
    gscale_all = sm("gscale_all", [P, NOWN])
    thr_all = sm("thr_all", [P, NOWN])
    dscale_all = sm("dscale_all", [P, NOWN])
    bk_cols = sm("bk_cols", [P, CD])
    bq_cols = sm("bq_cols", [P, CD])
    bq32_cols = sm("bq32_cols", [P, CD])
    bv_bf = sm("bv_bf", [1, D], BF16)
    onesr = sm("onesr", [1, P], BF16)
    negr = sm("negr", [1, P], BF16)
    onesc = sm("onesc", [P, 1], BF16)
    onescf = sm("onescf", [P, 1], F32)
    ones_rf = sm("ones_rf", [1, P], F32)
    ones11f = sm("ones11f", [1, 1], F32)
    c127 = sm("c127", [P, 1], F32)
    kmax11 = sm("kmax11", [1, 1], F32)
    tmax11 = sm("tmax11", [1, 1], F32)
    u_col = sm("u_col", [P, 1], F32)
    i0_col = sm("i0_col", [P, 1], F32)
    e1_col = sm("e1_col", [P, 1], F32)
    mk2 = sm("mk2", [1, 1], F32)
    wsum = sm("wsum", [P, 3], F32)
    part_col = sm("part_col", [P, 1], F32)
    cg_col = sm("cg_col", [P, 3], F32)
    sg_col = sm("sg_col", [P, 3], F32)
    cc_col = sm("cc_col", [P, 1], F32)
    linv_cols = sm("linv_cols", [P, 8], F32)
    hm_col = sm("hm_col", [P, 1], F32)
    tc_col = sm("tc_col", [P, 1], F32)
    hs_col = sm("hs_col", [P, 1], F32)
    d1_col = sm("d1_col", [P, 1], F32)
    cgv = sm("cgv", [1, 3], F32)

    PS = [nc.alloc_psum_tensor(f"ps{i}", [P, 1024], F32) for i in range(4)]

    E("vector", lambda e: e.memset(onesr[:], 1.0))
    E("vector", lambda e: e.memset(negr[:], -1.0))
    E("vector", lambda e: e.memset(onesc[:], 1.0))
    E("vector", lambda e: e.memset(onescf[:], 1.0))
    E("vector", lambda e: e.memset(ones_rf[:], 1.0))
    E("vector", lambda e: e.memset(ones11f[:], 1.0))
    E("vector", lambda e: e.memset(c127[:], 127.0))
    consts_done = E("vector", lambda e: e.memset(wsum[:], 0.0), inc=sVE)

    for wi, wT in ((0, wqT_h), (1, wkT_h), (2, wvT_h)):
        DMA("gpsimd", WQKV[:, wi, :, :],
            wT.rearrange("(c p) d -> p c d", p=P), inc=sGP)
    wqkv_done = sGP.n
    DMA("gpsimd", bk_cols[:], bk_h.rearrange("(c p) -> p c", p=P), inc=sGP,
        allow_slow_non_contiguous=True)
    DMA("gpsimd", bq_cols[:], bq_h.rearrange("(c p) -> p c", p=P), inc=sGP,
        allow_slow_non_contiguous=True)
    DMA("gpsimd", bv_bf[:], bv_h.rearrange("(o d) -> o d", o=1), inc=sGP)
    bias_done = sGP.n
    bq32_done = E("vector", lambda e: e.tensor_scalar(
        bq32_cols[:], bq_cols[:], 1.0 / 32.0, None, ALU.mult),
        waits=[(sGP, bias_done)], inc=sVE)

    strip_srcs = []
    for c in range(CD):
        for s in range(4):
            strip_srcs.append(wgT_h[c * P:(c + 1) * P, s * 1024:(s + 1) * 1024])
    for c in range(CD):
        for s in range(4):
            strip_srcs.append(wuT_h[c * P:(c + 1) * P, s * 1024:(s + 1) * 1024])
    for ic in range(NKC):
        strip_srcs.append(wdT_h[ic * P:(ic + 1) * P, :])
    NS = len(strip_srcs)
    p1_bufs = [P8F[:, 0:1024], P8F[:, 1024:2048]]
    p1_dma_cnt = [None] * NS
    p1_vec_cnt = [None] * NS
    p1 = {"dma": 0, "vec": 0}

    def p1_emit_dma():
        si = p1["dma"]
        if si >= NS:
            return
        w = []
        if si >= 2 and p1_vec_cnt[si - 2] is not None:
            w.append((sVE, p1_vec_cnt[si - 2]))
        p1_dma_cnt[si] = DMA("gpsimd", p1_bufs[si % 2], strip_srcs[si],
                             waits=w, inc=sGP)
        p1["dma"] += 1

    def p1_emit_vec():
        si = p1["vec"]
        if si >= NS or p1_dma_cnt[si] is None:
            return
        wcol = si // 32
        E("vector", lambda e, b=p1_bufs[si % 2]: e.tensor_reduce(
            part_col[:], b, AX.X, ALU.add, apply_absolute_value=True),
          waits=[(sGP, p1_dma_cnt[si])])
        p1_vec_cnt[si] = E("vector", lambda e, w=wcol: e.tensor_add(
            wsum[:, w:w + 1], wsum[:, w:w + 1], part_col[:]),
            waits=[(sVE, consts_done)], inc=sVE)
        p1["vec"] += 1

    p1_emit_dma()
    p1_emit_dma()

    xA4 = [A32[:, 8192:12288], A32[:, 12288:16384]]
    junkA = M8F[:, 1024:2048]
    xnb4 = [MSC8[:, 0:4096], None]
    xbq = [A32[:, 8192:9216], A32[:, 9216:10240]]
    xnf = [A32[:, 10240:11264], A32[:, 11264:12288]]
    vppad = A32[:, 12288:12800]
    lnpadA = A32[:, 12800:13312]
    rpadA = A32[:, 13312:13824]
    xib2 = [AR_PT[:, 28672:29696], AR_PT[:, 29696:30720]]
    xnb = [MSC8[:, 0:1024], MSC8[:, 1024:2048]]
    xib = [MSC8[:, 2048:3072], MSC8[:, 3072:4096]]

    cons = {0: [], 1: []}
    for k in range(NT // 4):
        par = k % 2
        ld = DMA("sync", xA4[par].rearrange("p (t d) -> p t d", t=4),
                 x_h[k * 4 * P:(k + 1) * 4 * P, :].rearrange(
                     "(t p) d -> p t d", p=P),
                 waits=cons[par], inc=sSY)
        last_sc = last_vc = None
        for j in range(4):
            i = 4 * k + j
            last_sc = E("scalar", lambda e, p=par, j=j, i=i: e.activation(
                junkA, xA4[p][:, j * 1024:(j + 1) * 1024], AF.Square,
                accum_out=s2_all[:, i:i + 1]),
                waits=([(sSY, ld)] if j == 0 else []), inc=sSC)
            last_vc = E("vector", lambda e, p=par, j=j, i=i: e.reduce_sum(
                s1_all[:, i:i + 1], xA4[p][:, j * 1024:(j + 1) * 1024],
                AX.X), waits=([(sSY, ld)] if j == 0 else []), inc=sVE)
        cons[par] = [(sSC, last_sc), (sVE, last_vc)]
    a1_sc = sSC.n
    a1_sc = sSC.n

    E("vector", lambda e: e.tensor_scalar(
        m_all[:], s1_all[:], 1.0 / D, None, ALU.mult),
      waits=[(sSC, a1_sc)])
    E("vector", lambda e: e.tensor_scalar(
        vp_all[:], s2_all[:], 1.0 / D, None, ALU.mult))
    E("vector", lambda e: e.tensor_copy(svp_all[:], m_all[:]))
    E("vector", lambda e: e.tensor_mul(t1_all[:], m_all[:], svp_all[:]))
    E("vector", lambda e: e.tensor_sub(vp_all[:], vp_all[:], t1_all[:]))
    E("vector", lambda e: e.tensor_scalar(
        vp_all[:], vp_all[:], 1e-5, None, ALU.add))
    E("vector", lambda e: e.memset(vppad[:], 1.0))
    a2v1 = E("vector", lambda e: e.tensor_copy(vppad[:, 0:NT], vp_all[:]),
             inc=sVE)
    E("scalar", lambda e: e.activation(lnpadA[:], vppad[:], AF.Ln),
      waits=[(sVE, a2v1)])
    a2s = E("scalar", lambda e: e.activation(rpadA[:], lnpadA[:], AF.Exp,
                                             scale=-0.5), inc=sSC)
    E("vector", lambda e: e.tensor_copy(r_all[:], rpadA[:, 0:NT]),
      waits=[(sSC, a2s)])
    E("vector", lambda e: e.tensor_mul(t1_all[:], vp_all[:], r_all[:]))
    E("vector", lambda e: e.tensor_mul(t1_all[:], t1_all[:], r_all[:]))
    E("vector", lambda e: e.tensor_scalar(
        t1_all[:], t1_all[:], -0.5, 1.5, ALU.mult, ALU.add))
    E("vector", lambda e: e.tensor_mul(r_all[:], r_all[:], t1_all[:]))
    E("vector", lambda e: e.tensor_mul(t1_all[:], m_all[:], r_all[:]))
    a2_done = E("vector", lambda e: e.tensor_scalar(
        bact_all[:], t1_all[:], -1.0, None, ALU.mult), inc=sVE)
    if DEBUG:
        DMA("sync", s_dbg[:, 7 * NT:8 * NT], r_all[:],
            waits=[(sVE, a2_done)], inc=sSY)
        DMA("gpsimd", s_dbg[:, 0:NT], s1_all[:], waits=[(sVE, a2_done)], inc=sGP)
        DMA("gpsimd", s_dbg[:, NT:2 * NT], s2_all[:], inc=sGP)
        DMA("gpsimd", s_dbg[:, 2 * NT:3 * NT], r_all[:], inc=sGP)
        DMA("gpsimd", s_dbg[:, 3 * NT:4 * NT], bact_all[:], inc=sGP)
        DMA("gpsimd", s_dbg[:, 4 * NT:5 * NT], vp_all[:], inc=sGP)
        DMA("gpsimd", s_dbg[:, 5 * NT:6 * NT], svp_all[:], inc=sGP)

    xn_store_cnt = {}
    xnb4_free = 0
    for k in range(NT // 4):
        par = k % 2
        ld = DMA("sync", xA4[par].rearrange("p (t d) -> p t d", t=4),
                 x_h[k * 4 * P:(k + 1) * 4 * P, :].rearrange(
                     "(t p) d -> p t d", p=P),
                 waits=cons[par], inc=sSY)
        last_sc = None
        for j in range(4):
            i = 4 * k + j
            w2 = [(sVE, a2_done)]
            if j == 0:
                w2.append((sSY, ld))
                w2.append((sSY, xnb4_free))
            last_sc = E("scalar", lambda e, p=par, j=j, i=i: e.activation(
                xnb4[0][:, j * 1024:(j + 1) * 1024],
                xA4[p][:, j * 1024:(j + 1) * 1024], AF.Identity,
                bias=bact_all[:, i:i + 1], scale=r_all[:, i:i + 1]),
                waits=w2, inc=sSC, drain=(j == 3))
        DMA("sync",
            xn_h[k * 4 * P:(k + 1) * 4 * P, :].rearrange(
                "(t p) d -> p t d", p=P),
            xnb4[0].rearrange("p (t d) -> p t d", t=4),
            waits=[(sSC, last_sc)], inc=sSY)
        xnb4_free = sSY.n
        for j in range(4):
            xn_store_cnt[4 * k + j] = sSY.n
        cons[par] = [(sSC, last_sc)]
    a3_stores = sSY.n

    vt_bf = [STR8[:, 0:1024], STR8[:, 1024:2048]]
    qt_bf = [STR8[:, 2048:3072], STR8[:, 3072:4096]]
    evacK = {0: 0, 1: 0}
    evacV = {0: 0, 1: 0}
    vt_free = {0: 0, 1: 0}
    qt_free = {0: 0, 1: 0}
    xnq_mm_done = {0: 0, 1: 0}

    for qr in range(4):
        qp = qr % 2
        tq = None
        for c in range(CD):
            w = []
            if c == 0:
                w = [(sSY, xn_store_cnt[qr * 8 + 7]),
                     (sTE, xnq_mm_done[qp]),
                     (sSC, sSC.n)]
            tq = DMA("sync", XNQ[:, qp * 8 + c, :],
                     xn_h[qr * 1024:(qr + 1) * 1024, c * P:(c + 1) * P],
                     waits=w, inc=sSY, transpose=True)
        xnq_done = tq

        for o in range(CD):
            par = o % 2
            ps = PS[par]
            for c in range(CD):
                w = []
                if c == 0:
                    w = [(sSY, xnq_done), (sGP, wqkv_done), (sSC, evacK[par])]
                lhsT = WQKV[:, 1, c, o * P:(o + 1) * P]
                E("tensor", lambda e, l=lhsT, p=ps, c=c, qp=qp:
                  e.matmul(p[:, 0:512], l, XNQ[:, qp * 8 + c, 0:512],
                           start=(c == 0), stop=(c == CD - 1),
                           skip_group_check=True), waits=w)
                E("tensor", lambda e, l=lhsT, p=ps, c=c, qp=qp:
                  e.matmul(p[:, 512:1024], l, XNQ[:, qp * 8 + c, 512:1024],
                           start=(c == 0), stop=(c == CD - 1),
                           skip_group_check=True),
                  inc=(sTE if c == CD - 1 else None))
            grp = sTE.n
            evacK[par] = E("scalar", lambda e, p=ps, o=o, qr=qr: e.activation(
                KF[:, o, qr * 1024:(qr + 1) * 1024], p[:], AF.Identity,
                bias=bk_cols[:, o:o + 1]),
                waits=[(sTE, grp), (sGP, bias_done)], inc=sSC, drain=False)

        for tb in range(CD):
            par = tb % 2
            ps = PS[2 + par]
            for c in range(CD):
                w = []
                if c == 0:
                    w = [(sSY, xnq_done), (sGP, bias_done), (sSC, evacV[par]),
                         (sVE, consts_done)]
                lhsT = XNQ[:, qp * 8 + c, tb * P:(tb + 1) * P]
                E("tensor", lambda e, l=lhsT, p=ps, c=c:
                  e.matmul(p[:, 0:512], l, WQKV[:, 2, c, 0:512],
                           start=(c == 0), stop=False,
                           skip_group_check=True), waits=w)
                E("tensor", lambda e, l=lhsT, p=ps, c=c:
                  e.matmul(p[:, 512:1024], l, WQKV[:, 2, c, 512:1024],
                           start=(c == 0), stop=False,
                           skip_group_check=True))
            E("tensor", lambda e, p=ps: e.matmul(
                p[:, 0:512], onesr[0:1, :], bv_bf[0:1, 0:512],
                start=False, stop=True, skip_group_check=True))
            E("tensor", lambda e, p=ps: e.matmul(
                p[:, 512:1024], onesr[0:1, :], bv_bf[0:1, 512:1024],
                start=False, stop=True, skip_group_check=True), inc=sTE)
            grp = sTE.n
            evacV[par] = E("scalar", lambda e, p=ps, par=par: e.copy(
                vt_bf[par], p[:]), waits=[(sTE, grp), (sSY, vt_free[par])],
                inc=sSC, drain=False)
            vt_free[par] = DMA(
                "sync", v_h[qr * 1024 + tb * P: qr * 1024 + (tb + 1) * P, :],
                vt_bf[par], waits=[(sSC, evacV[par])], inc=sSY)

        if qr < 2:
            for o in range(CD):
                par = o % 2
                ps = PS[par]
                for c in range(CD):
                    w = []
                    if c == 0:
                        w = [(sSY, xnq_done), (sSC, evacK[par]),
                             (sVE, bq32_done)]
                    lhsT = WQKV[:, 0, c, o * P:(o + 1) * P]
                    E("tensor", lambda e, l=lhsT, p=ps, c=c, qp=qp:
                      e.matmul(p[:, 0:512], l, XNQ[:, qp * 8 + c, 0:512],
                               start=(c == 0), stop=(c == CD - 1),
                               skip_group_check=True), waits=w)
                    E("tensor", lambda e, l=lhsT, p=ps, c=c, qp=qp:
                      e.matmul(p[:, 512:1024], l,
                               XNQ[:, qp * 8 + c, 512:1024],
                               start=(c == 0), stop=(c == CD - 1),
                               skip_group_check=True),
                      inc=(sTE if c == CD - 1 else None))
                grp = sTE.n
                evacK[par] = E("scalar", lambda e, p=ps, o=o, par=par:
                               e.activation(qt_bf[par], p[:], AF.Identity,
                                            bias=bq32_cols[:, o:o + 1],
                                            scale=1.0 / 32.0),
                               waits=[(sTE, grp), (sSY, qt_free[par])],
                               inc=sSC, drain=False)
                qt_free[par] = DMA(
                    "sync",
                    q_h[o * P:(o + 1) * P, qr * 1024:(qr + 1) * 1024],
                    qt_bf[par], waits=[(sSC, evacK[par])], inc=sSY)
        xnq_mm_done[qp] = sTE.n

        for _ in range(24):
            p1_emit_vec()
            p1_emit_dma()
    while p1["vec"] < NS:
        p1_emit_vec()
        p1_emit_dma()
    p1_done = sVE.n
    b_te = sTE.n
    b_sc = sSC.n
    b_sy = sSY.n

    xq_free = {}
    xib2_free = {}
    a3b_gp = sGP.n
    for i in range(NOWN):
        par = i % 2
        wld = list(xq_free.get(par, []))
        if i < 2:
            wld += cons[0] + cons[1]
        ldq = DMA("gpsimd", xbq[par], x_h[i * P:(i + 1) * P, :],
                  waits=wld, inc=sGP)
        wsq2 = [(sGP, ldq), (sVE, a2_done)] + list(xq_free.get(par, []))
        scq = E("scalar", lambda e, p=par, i=i: e.activation(
            xnf[p], xbq[p], AF.Identity, bias=bact_all[:, i:i + 1],
            scale=r_all[:, i:i + 1]), waits=wsq2,
            inc=sSC, drain=False)
        E("vector", lambda e, p=par, i=i: e.reduce_max(
            ax_all[:, i:i + 1], xnf[p], AX.X, apply_absolute_value=True),
          waits=[(sSC, scq)])
        E("vector", lambda e, i=i: e.tensor_scalar(
            t_all[:, i:i + 1], ax_all[:, i:i + 1], 1e-5, None, ALU.max))
        E("vector", lambda e, i=i: e.tensor_scalar(
            isx_all[:, i:i + 1], t_all[:, i:i + 1], 1.0 / 127.0,
            None, ALU.mult))
        vq = E("vector", lambda e, i=i: e.tensor_copy(
            vppad[:, 0:1], t_all[:, i:i + 1]), inc=sVE)
        E("scalar", lambda e: e.activation(lnpadA[:], vppad[:], AF.Ln),
          waits=[(sVE, vq)])
        scx = E("scalar", lambda e: e.activation(
            rpadA[:], lnpadA[:], AF.Exp, scale=-1.0), inc=sSC)
        E("vector", lambda e, i=i: e.tensor_tensor(
            e1_col[:], t_all[:, i:i + 1], rpadA[:, 0:1], ALU.mult),
          waits=[(sSC, scx)])
        E("vector", lambda e: e.tensor_scalar(
            e1_col[:], e1_col[:], -1.0, 2.0, ALU.mult, ALU.add))
        E("vector", lambda e: e.tensor_tensor(
            e1_col[:], rpadA[:, 0:1], e1_col[:], ALU.mult))
        sxv = E("vector", lambda e, i=i: e.tensor_scalar(
            sx_all[:, i:i + 1], e1_col[:], 127.0, None, ALU.mult), inc=sVE)
        E("vector", lambda e, p=par, i=i: e.tensor_tensor(
            xnf[p], xnf[p], sx_all[:, i:i + 1].broadcast_to((P, 1024)),
            ALU.mult), waits=[(sVE, sxv)], drain=False)
        wib = [(sSY, xib2_free[par])] if par in xib2_free else []
        xiv = E("vector", lambda e, p=par: e.tensor_scalar(
            xib2[p], xnf[p], RC, RC, ALU.add, ALU.subtract),
            waits=wib, inc=sVE, drain=False)
        xq_free[par] = [(sVE, xiv)]
        DMA("sync", xi_h[i * P:(i + 1) * P, :], xib2[par],
            waits=[(sVE, xiv)], inc=sSY)
        xib2_free[par] = sSY.n
    a3b_done = sVE.n
    a3b_gp = sGP.n
    a3b_sy = sSY.n

    if DEBUG:
        for c in range(CD):
            DMA("gpsimd", k_dbg[c * P:(c + 1) * P, :], KF[:, c, :],
                waits=[(sSC, b_sc)], inc=sGP)

    accb = AWB[:, 0:4096]
    tmpb = AWB[:, 4096:8192]
    knr = AWF[:, 4096:8192]
    ks0 = E("scalar", lambda e: e.activation(accb, KF[:, 0, :], AF.Square),
            waits=[(sSC, b_sc), (sTE, b_te)], inc=sSC, drain=False)
    kn_v = None
    for c in range(1, CD):
        ksc = E("scalar", lambda e, c=c: e.activation(
            tmpb, KF[:, c, :], AF.Square),
            waits=([(sVE, kn_v)] if kn_v else []), inc=sSC, drain=False)
        kn_v = E("vector", lambda e: e.tensor_add(accb, accb, tmpb),
                 waits=[(sSC, ksc)], inc=sVE)
    km_init = E("vector", lambda e: e.memset(kmax11[:], 0.0), inc=sVE)
    kn_prev = km_init
    for j in range(8):
        mmj = E("tensor", lambda e, j=j: e.matmul(
            PS[0][0:1, 0:512], onesc[:, 0:1], accb[:, j * 512:(j + 1) * 512],
            start=True, stop=True, skip_group_check=True),
            waits=[(sVE, kn_v if j == 0 else kn_prev),
                   (sSC, b_sc)], inc=sTE)
        E("vector", lambda e: e.reduce_max(
            tmax11[:], PS[0][0:1, 0:512], AX.X), waits=[(sTE, mmj)])
        kn_prev = E("vector", lambda e: e.tensor_max(
            kmax11[:], kmax11[:], tmax11[:]), inc=sVE)
    mk2_done = E("vector", lambda e: e.tensor_scalar(
        mk2[:], kmax11[:], 1.01, None, ALU.mult), inc=sVE)

    c_row = M8F[0:1, 0:1024]
    c_bf = MSC8[0:1, 2048:3072]
    at_bf = [MSC8[:, 3072:3584], MSC8[:, 3584:4096]]
    u_row = S8F[0:1, 0:1024]
    linv_row = S8F[0:1, 1024:2048]
    accq = AWB[:, 0:1024]
    tmpq = AWB[:, 1024:2048]

    pv_te = b_te
    at_free = {0: 0, 1: 0}
    exp_cnt = {}
    linvc_prev = 0
    for hq in range(2):
        qld = None
        for c in range(CD):
            w = []
            if c == 0:
                w = [(sSY, b_sy), (sTE, pv_te), (sSC, sSC.n)]
            qld = DMA("sync", Q24[:, 16 + c, :],
                      q_h[c * P:(c + 1) * P, hq * 1024:(hq + 1) * 1024],
                      waits=w, inc=sSY)
        E("scalar", lambda e: e.activation(
            accq, Q24[:, 16, :], AF.Square), waits=[(sSY, qld)],
          drain=False)
        qn_v = None
        for c in range(1, CD):
            qsc = E("scalar", lambda e, c=c: e.activation(
                tmpq, Q24[:, 16 + c, :], AF.Square),
                waits=([(sVE, qn_v)] if qn_v else []), inc=sSC, drain=False)
            qn_v = E("vector", lambda e: e.tensor_add(accq, accq, tmpq),
                     waits=[(sSC, qsc)], inc=sVE)
        qm1 = E("tensor", lambda e: e.matmul(
            PS[0][0:1, 0:512], onesc[:, 0:1], accq[:, 0:512],
            start=True, stop=True, skip_group_check=True),
            waits=[(sVE, qn_v)], inc=sTE)
        qm2 = E("tensor", lambda e: e.matmul(
            PS[0][0:1, 512:1024], onesc[:, 0:1], accq[:, 512:1024],
            start=True, stop=True, skip_group_check=True), inc=sTE)
        yq = E("vector", lambda e: e.tensor_scalar(
            PS[0][0:1, 0:1024], PS[0][0:1, 0:1024], mk2[0:1, 0:1],
            None, ALU.mult),
            waits=[(sTE, qm2), (sVE, mk2_done)], inc=sVE)
        lnq = E("scalar", lambda e: e.activation(
            u_row, PS[0][0:1, 0:1024], AF.Ln), waits=[(sVE, yq)], inc=sSC)
        c_s = E("scalar", lambda e: e.activation(
            c_row, u_row, AF.Exp, scale=0.5), waits=[(sSC, lnq)], inc=sSC)
        cbf_done = E("vector", lambda e: e.tensor_copy(c_bf, c_row),
                     waits=[(sSC, c_s)], inc=sVE)

        for kc in range(NKC):
            par = kc % 2
            ps = PS[par]
            for c in range(CD):
                w = []
                if c == 0:
                    w = [(sVE, cbf_done),
                         (sSC, exp_cnt.get((hq, kc - 2),
                                           exp_cnt.get((hq - 1, NKC - 2 + kc),
                                                       0)))]
                    if hq == 0 and kc == 16:
                        w += [(sVE, a3b_done), (sSY, a3b_sy)]
                lhsT = KF[:, c, kc * P:(kc + 1) * P]
                E("tensor", lambda e, l=lhsT, p=ps, c=c:
                  e.matmul(p[:, 0:512], l, Q24[:, 16 + c, 0:512],
                           start=(c == 0), stop=False,
                           skip_group_check=True), waits=w)
                E("tensor", lambda e, l=lhsT, p=ps, c=c:
                  e.matmul(p[:, 512:1024], l, Q24[:, 16 + c, 512:1024],
                           start=(c == 0), stop=False,
                           skip_group_check=True))
            E("tensor", lambda e, p=ps: e.matmul(
                p[:, 0:512], negr[0:1, :], c_bf[0:1, 0:512],
                start=False, stop=True, skip_group_check=True))
            E("tensor", lambda e, p=ps: e.matmul(
                p[:, 512:1024], negr[0:1, :], c_bf[0:1, 512:1024],
                start=False, stop=True, skip_group_check=True), inc=sTE)
            grp = sTE.n
            exp_cnt[(hq, kc)] = E("scalar", lambda e, p=ps, kc=kc:
                                  e.activation(PT[:, kc, :], p[:], AF.Exp),
                                  waits=[(sTE, grp)], inc=sSC, drain=False)
            wl = [(sSC, exp_cnt[(hq, kc)])]
            if kc == 0:
                wl.append((sVE, linvc_prev))
            E("tensor", lambda e, kc=kc: e.matmul(
                PS[2][0:1, 0:512], onesc[:, 0:1], PT[:, kc, 0:512],
                start=(kc == 0), stop=(kc == NKC - 1),
                skip_group_check=True), waits=wl)
            E("tensor", lambda e, kc=kc: e.matmul(
                PS[2][0:1, 512:1024], onesc[:, 0:1], PT[:, kc, 512:1024],
                start=(kc == 0), stop=(kc == NKC - 1),
                skip_group_check=True),
              inc=(sTE if kc == NKC - 1 else None))
        l_done = sTE.n
        scores_te_mark = l_done
        lnl = E("scalar", lambda e: e.activation(
            u_row, PS[2][0:1, 0:1024], AF.Ln),
            waits=[(sTE, l_done)], inc=sSC)
        lv = E("scalar", lambda e: e.activation(
            linv_row, u_row, AF.Exp, scale=-1.0),
            waits=[(sSC, lnl)], inc=sSC)
        for qt in range(8):
            E("tensor", lambda e, qt=qt: e.matmul(
                PS[2][:, qt:qt + 1], linv_row[0:1, qt * P:(qt + 1) * P],
                ones11f[0:1, 0:1], start=True, stop=True,
                skip_group_check=True),
              waits=([(sSC, lv)] if qt == 0 else []),
              inc=(sTE if qt == 7 else None))
        linvc = E("vector", lambda e: e.tensor_copy(
            linv_cols[:], PS[2][:, 0:8]), waits=[(sTE, sTE.n)], inc=sVE)
        linvc_prev = linvc

        for dh in range(2):
            vld = None
            varea_guard = sTE.n
            for g in range(8):
                w = []
                if g == 0:
                    w = [(sTE, varea_guard), (sSY, b_sy)]
                vld = DMA(
                    "sync", V48[:, g * 4:(g + 1) * 4, :],
                    v_h.rearrange("(k p) d -> p k d", p=P)[
                        :, g * 4:(g + 1) * 4, dh * 512:(dh + 1) * 512],
                    waits=w, inc=sSY)
            for qt in range(8):
                par = qt % 2
                psO = PS[3][:, par * 512:(par + 1) * 512]
                for kc in range(NKC):
                    w = []
                    if kc == 0:
                        w = [(sSY, vld), (sVE, linvc),
                             (sSC, at_free.get(par, 0))]
                    E("tensor", lambda e, p=psO, kc=kc, qt=qt:
                      e.matmul(p, PT[:, kc, qt * P:(qt + 1) * P],
                               V48[:, kc, :], start=(kc == 0),
                               stop=(kc == NKC - 1),
                               skip_group_check=True),
                      waits=w, inc=(sTE if kc == NKC - 1 else None))
                grp = sTE.n
                ev = E("scalar", lambda e, p=psO, par=par, qt=qt:
                       e.activation(at_bf[par], p, AF.Identity,
                                    scale=linv_cols[:, qt:qt + 1]),
                       waits=[(sTE, grp)], inc=sSC, drain=False)
                at_free[par] = ev
                row0 = hq * 1024 + qt * P
                DMA("sync", at_h[row0:row0 + P, dh * 512:(dh + 1) * 512],
                    at_bf[par], waits=[(sSC, ev)], inc=sSY)
        if hq == 1:
            scores_te = scores_te_mark
        pv_te = sTE.n
    d_sy = sSY.n
    d_te = sTE.n

    d_ve = sVE.n
    d_gc = sGC.n
    d_sc = sSC.n
    cg_done = {}
    cv_prev = 0
    csc_prev = 0
    for w in range(3):
        mmn = E("tensor", lambda e, w=w: e.matmul(
            PS[0][0:1, 0:1], wsum[:, w:w + 1], onescf[:, 0:1],
            start=True, stop=True, skip_group_check=True),
            waits=([(sVE, p1_done), (sTE, scores_te),
                    (sSC, exp_cnt[(1, NKC - 1)])]
                   if w == 0 else [(sVE, cv_prev)]), inc=sTE)
        cv = E("vector", lambda e, w=w: e.tensor_scalar(
            cgv[0:1, w:w + 1], PS[0][0:1, 0:1],
            1.0 / (D * INNER), 1e-5, ALU.mult, ALU.max),
            waits=[(sTE, mmn)], inc=sVE)
        cv_prev = cv
        mmb = E("tensor", lambda e, w=w: e.matmul(
            PS[1][:, 0:1], ones_rf[0:1, :], cgv[0:1, w:w + 1],
            start=True, stop=True, skip_group_check=True),
            waits=[(sVE, cv), (sVE, csc_prev)], inc=sTE)
        csc = E("vector", lambda e, w=w: e.tensor_copy(
            cg_col[:, w:w + 1], PS[1][:, 0:1]), waits=[(sTE, mmb)])
        csc2 = E("vector", lambda e, w=w: e.tensor_copy(
            vppad[:, 0:1], cg_col[:, w:w + 1]), inc=sVE)
        csc_prev = csc2
        E("scalar", lambda e: e.activation(
            lnpadA[:], vppad[:], AF.Ln), waits=[(sVE, csc2)])
        sgx = E("scalar", lambda e: e.activation(
            rpadA[:], lnpadA[:], AF.Exp, scale=-1.0), inc=sSC)
        E("vector", lambda e, w=w: e.tensor_tensor(
            e1_col[:], cg_col[:, w:w + 1], rpadA[:, 0:1], ALU.mult),
          waits=[(sSC, sgx)])
        E("vector", lambda e: e.tensor_scalar(
            e1_col[:], e1_col[:], -1.0, 2.0, ALU.mult, ALU.add))
        cg_done[w] = E("vector", lambda e, w=w: e.tensor_tensor(
            sg_col[:, w:w + 1], rpadA[:, 0:1], e1_col[:], ALU.mult), inc=sVE)
    E("vector", lambda e: e.tensor_tensor(
        gscale_all[:], isx_all[:], cg_col[:, 0:1].broadcast_to((P, NOWN)),
        ALU.mult), waits=[(sVE, cg_done[2])])
    E("vector", lambda e: e.tensor_tensor(
        thr_all[:], sx_all[:], sg_col[:, 1:2].broadcast_to((P, NOWN)),
        ALU.mult))
    E("vector", lambda e: e.tensor_scalar(
        thr_all[:], thr_all[:], 1e-5, None, ALU.mult))
    E("vector", lambda e: e.tensor_tensor(
        cc_col[:], cg_col[:, 1:2], cg_col[:, 2:3], ALU.mult))
    eprol = E("vector", lambda e: e.tensor_scalar(
        cc_col[:], cc_col[:], 1.0 / 127.0, None, ALU.mult), inc=sVE)
    eprol_sc = sSC.n

    q1 = [P8F[:, 0:1024], P8F[:, 1024:2048]]
    q1_free = {0: 0, 1: 0}
    wq_ready = {}
    ea0_i = 0
    for s in range(4):
        for w, dst, wsrc, area_te in ((0, WG, wgT_h, scores_te),
                                      (1, WU, wuT_h, d_te)):
            for c in range(CD):
                par = ea0_i % 2
                ea0_i += 1
                wld = [(sVE, q1_free[par])]
                if ea0_i <= 2:
                    wld.append((sVE, p1_done))
                ld = DMA("gpsimd", q1[par],
                         wsrc[c * P:(c + 1) * P, s * 1024:(s + 1) * 1024],
                         waits=wld, inc=sGP)
                E("vector", lambda e, par=par, w=w: e.tensor_tensor(
                    q1[par], q1[par],
                    sg_col[:, w:w + 1].broadcast_to((P, 1024)), ALU.mult),
                    waits=[(sGP, ld), (sVE, cg_done[w]), (sTE, area_te)],
                    drain=False)
                E("vector", lambda e, par=par: e.tensor_scalar(
                    q1[par], q1[par], RC, RC, ALU.add, ALU.subtract),
                  drain=False)
                q1_free[par] = E(
                    "vector", lambda e, par=par, d=dst, c=c, s=s:
                    e.tensor_scalar(d[:, c, s * 1024:(s + 1) * 1024],
                                    q1[par], 1.0, -1.0, ALU.min, ALU.max),
                    inc=sVE)
                wq_ready[(w, c, s)] = q1_free[par]
    ea0_ve = sVE.n

    xld = None
    for c in range(CD):
        w = []
        if c == 0:
            w = [(sTE, d_te), (sSY, d_sy), (sVE, d_ve), (sGC, d_gc)]
        xld = DMA("sync", XI12[:, c, 0:2048],
                  xi_h[0:TOWN, c * P:(c + 1) * P], waits=w, inc=sSY,
                  transpose=True)
    H = AWF[:, 8192:12288]
    gsil = S8F[:, 0:1024]
    hib = MSC8[:]
    gu_ev = {}
    hi_st_last = 0
    hi_tile_gp = {}
    dsc_done = {}
    for t in range(NOWN):
        for iq in range(4):
            gi = t * 4 + iq
            pg = PS[0] if iq % 2 == 0 else PS[2]
            pu = PS[1] if iq % 2 == 0 else PS[3]
            for c in range(CD):
                w = []
                if c == 0:
                    w = [(sSY, xld)]
                    if gi == 0:
                        w.append((sSC, eprol_sc))
                    pe = gu_ev.get(gi - 2)
                    if pe:
                        w += [(sSC, pe[0]), (sVE, pe[1])]
                    w.append((sVE, wq_ready[(1, CD - 1, iq)]))
                lhsT = XI12[:, c, t * P:(t + 1) * P]
                E("tensor", lambda e, l=lhsT, p=pg, c=c, iq=iq:
                  e.matmul(p[:, 0:512], l,
                           WG[:, c, iq * 1024:iq * 1024 + 512],
                           start=(c == 0), stop=(c == CD - 1),
                           skip_group_check=True), waits=w)
                E("tensor", lambda e, l=lhsT, p=pg, c=c, iq=iq:
                  e.matmul(p[:, 512:1024], l,
                           WG[:, c, iq * 1024 + 512:(iq + 1) * 1024],
                           start=(c == 0), stop=(c == CD - 1),
                           skip_group_check=True))
                E("tensor", lambda e, l=lhsT, p=pu, c=c, iq=iq:
                  e.matmul(p[:, 0:512], l,
                           WU[:, c, iq * 1024:iq * 1024 + 512],
                           start=(c == 0), stop=(c == CD - 1),
                           skip_group_check=True))
                E("tensor", lambda e, l=lhsT, p=pu, c=c, iq=iq:
                  e.matmul(p[:, 512:1024], l,
                           WU[:, c, iq * 1024 + 512:(iq + 1) * 1024],
                           start=(c == 0), stop=(c == CD - 1),
                           skip_group_check=True),
                  inc=(sTE if c == CD - 1 else None))
            grp = sTE.n
            wsil = [(sTE, grp), (sVE, eprol)]
            if gi >= 1:
                wsil.append((sVE, gu_ev[gi - 1][1]))
            g_sc = E("scalar", lambda e, p=pg, t=t: e.activation(
                gsil, p[:], AF.Silu, scale=gscale_all[:, t:t + 1]),
                waits=wsil, inc=sSC, drain=False)
            h_vc = E("vector", lambda e, p=pu, iq=iq: e.tensor_tensor(
                H[:, iq * 1024:(iq + 1) * 1024], gsil, p[:], ALU.mult),
                waits=[(sSC, g_sc)], inc=sVE, drain=False)
            gu_ev[gi] = (g_sc, h_vc)
        E("vector", lambda e: e.reduce_max(
            hm_col[:], H, AX.X, apply_absolute_value=True))
        E("vector", lambda e, t=t: e.tensor_tensor(
            tc_col[:], hm_col[:], thr_all[:, t:t + 1], ALU.max))
        tcv = E("vector", lambda e: e.tensor_copy(
            gsil[:, 0:1], tc_col[:]), inc=sVE)
        E("scalar", lambda e: e.activation(
            gsil[:, 512:1024], gsil[:, 0:512], AF.Ln), waits=[(sVE, tcv)])
        hsx = E("scalar", lambda e: e.activation(
            gsil[:, 0:512], gsil[:, 512:1024], AF.Exp, scale=-1.0), inc=sSC)
        E("vector", lambda e: e.tensor_tensor(
            e1_col[:], tc_col[:], gsil[:, 0:1], ALU.mult), waits=[(sSC, hsx)])
        E("vector", lambda e: e.tensor_scalar(
            e1_col[:], e1_col[:], -1.0, 2.0, ALU.mult, ALU.add))
        E("vector", lambda e: e.tensor_tensor(
            e1_col[:], gsil[:, 0:1], e1_col[:], ALU.mult))
        E("vector", lambda e: e.tensor_scalar(
            hs_col[:], e1_col[:], 127.0, None, ALU.mult))
        E("vector", lambda e, t=t: e.tensor_tensor(
            d1_col[:], tc_col[:], isx_all[:, t:t + 1], ALU.mult))
        dsc_done[t] = E("vector", lambda e, t=t: e.tensor_tensor(
            dscale_all[:, t:t + 1], d1_col[:], cc_col[:], ALU.mult), inc=sVE)
        for s4 in range(4):
            w = [(sGP, hi_st_last)] if s4 == 0 else []
            if s4 == 0 and t == 0:
                w.append((sSY, d_sy))
            E("vector", lambda e, s4=s4: e.tensor_tensor(
                H[:, s4 * 1024:(s4 + 1) * 1024],
                H[:, s4 * 1024:(s4 + 1) * 1024],
                hs_col[:].broadcast_to((P, 1024)), ALU.mult), waits=w,
              drain=False)
            E("vector", lambda e, s4=s4: e.tensor_scalar(
                hib[:, s4 * 1024:(s4 + 1) * 1024],
                H[:, s4 * 1024:(s4 + 1) * 1024], RC, RC,
                ALU.add, ALU.subtract),
              inc=(sVE if s4 == 3 else None), drain=(s4 == 3))
        hv = sVE.n
        hi_st_last = DMA("gpsimd", hi_h[t * P:(t + 1) * P, :], hib[:],
                         waits=[(sVE, hv)], inc=sGP)
        hi_tile_gp[t] = hi_st_last
    ea1_te = sTE.n
    ea1_gp = sGP.n
    ea1_sc = sSC.n
    ea1_ve = sVE.n

    atb = [AWB[:, 0:1024], AWB[:, 1024:2048]]
    xb = [AWF[:, 1024:2048], AWF[:, 2048:3072]]
    otb = [AWF[:, 3072:4096], AWF[:, 4096:5120]]
    oub = [AWF[:, 5120:6144], AWF[:, 6144:7168]]
    wdb = [P8F[:, 0:1024], P8F[:, 1024:2048]]
    wd_ready = {}
    q2ebb = S8F[:, 0:1024]
    wdb_free = {0: 0, 1: 0}
    ld_free = {0: [], 1: []}
    ot_ev = {0: 0, 1: 0}
    ou_done = {0: 0, 1: 0}
    down_te_half = ea1_te
    out_total = 0
    for th in range(2):
        hld = None
        for ic in range(NKC):
            w = []
            if ic == 0:
                w = [(sGP, hi_tile_gp[th * 8 + 7]), (sTE, down_te_half)]
            hld = DMA("sync", HFM[:, ic, :],
                      hi_h[th * 1024:(th + 1) * 1024, ic * P:(ic + 1) * P],
                      waits=w, inc=sSY, transpose=True)
        if th == 0:
            for ic in range(NKC):
                par = ic % 2
                wld = [(sVE, wdb_free[par])]
                if ic < 2:
                    wld.append((sVE, ea0_ve))
                ld = DMA("gpsimd", wdb[par],
                         wdT_h[ic * P:(ic + 1) * P, :],
                         waits=wld, inc=sGP)
                E("vector", lambda e, par=par: e.tensor_tensor(
                    wdb[par], wdb[par],
                    sg_col[:, 2:3].broadcast_to((P, 1024)), ALU.mult),
                  waits=[(sGP, ld)], drain=False)
                E("vector", lambda e, par=par: e.tensor_scalar(
                    wdb[par], wdb[par], RC, RC, ALU.add, ALU.subtract),
                  drain=False)
                wdb_free[par] = E(
                    "vector", lambda e, par=par, ic=ic: e.tensor_scalar(
                        WD[:, ic, :], wdb[par], 1.0, -1.0, ALU.min, ALU.max),
                    inc=sVE)
                wd_ready[ic] = wdb_free[par]
        for qt in range(8):
            tile = th * 8 + qt
            par = qt % 2
            psD = PS[par]
            ald = DMA("gpsimd", atb[par],
                      at_h[tile * P:(tile + 1) * P, :],
                      waits=list(ld_free[par]) + [(sSY, d_sy)], inc=sGP)
            xldb = DMA("gpsimd", xb[par], x_h[tile * P:(tile + 1) * P, :],
                       waits=ld_free[par], inc=sGP)
            for ic in range(NKC):
                w = []
                if ic == 0:
                    w = [(sSY, hld), (sSC, ot_ev[par])]
                    if th == 0 and qt < 2:
                        w += [(sSC, ea1_sc), (sVE, ea1_ve)]
                if th == 0:
                    w.append((sVE, wd_ready[ic]))
                lhsT = HFM[:, ic, qt * P:(qt + 1) * P]
                E("tensor", lambda e, l=lhsT, p=psD, ic=ic:
                  e.matmul(p[:, 0:512], l, WD[:, ic, 0:512],
                           start=(ic == 0), stop=(ic == NKC - 1),
                           skip_group_check=True), waits=w)
                E("tensor", lambda e, l=lhsT, p=psD, ic=ic:
                  e.matmul(p[:, 512:1024], l, WD[:, ic, 512:1024],
                           start=(ic == 0), stop=(ic == NKC - 1),
                           skip_group_check=True),
                  inc=(sTE if ic == NKC - 1 else None))
            grp = sTE.n
            ot_ev[par] = E("scalar", lambda e, p=psD, par=par, tile=tile:
                           e.activation(otb[par], p[:], AF.Identity,
                                        scale=dscale_all[:, tile:tile + 1]),
                           waits=[(sTE, grp), (sVE, dsc_done[tile])],
                           inc=sSC, drain=False)
            E("vector", lambda e, par=par: e.tensor_add(
                otb[par], otb[par], atb[par]),
              waits=[(sSC, ot_ev[par]), (sGP, ald), (sVE, ou_done[par])],
              drain=False)
            ov = E("vector", lambda e, par=par: e.tensor_add(
                oub[par], otb[par], xb[par]),
                waits=[(sGP, xldb)], inc=sVE, drain=False)
            ou_done[par] = ov
            ost = DMA("sync", out_h[tile * P:(tile + 1) * P, :], oub[par],
                      waits=[(sVE, ov)], inc=sSY)
            ld_free[par] = [(sVE, ov)]
        down_te_half = sTE.n
    out_total = sSY.n

    E("sync", lambda e: e.nop(), waits=[(sSY, out_total)])
    E("gpsimd", lambda e: e.nop(), waits=[(sGP, sGP.n)])

    for s in bd.sems:
        s.handle = nc.alloc_semaphore(name=s.name)
    with nc.Block() as block:
        def mk(name):
            def run(eng):
                for fn, waits, inc, step in bd.streams[name]:
                    for (s, v) in waits:
                        eng.wait_ge(s.handle, v)
                    ins = fn(eng)
                    if inc is not None:
                        ins.then_inc(inc.handle, step)
            return run
        block.sync(mk("sync"))
        block.gpsimd(mk("gpsimd"))
        block.tensor(mk("tensor"))
        block.vector(mk("vector"))
        block.scalar(mk("scalar"))
    return nc


_NC_CACHE = None


def kernel(L0_attn_x, L0_attn_Wq, L0_attn_bq, L0_attn_Wk, L0_attn_bk,
           L0_attn_Wv, L0_attn_bv, L0_ffn_w_gate, L0_ffn_w_up,
           L0_ffn_w_down):
    global _NC_CACHE
    x = np.ascontiguousarray(np.asarray(L0_attn_x, dtype=np.float32))
    wq = np.asarray(L0_attn_Wq, dtype=np.float32)
    wk = np.asarray(L0_attn_Wk, dtype=np.float32)
    wv = np.asarray(L0_attn_Wv, dtype=np.float32)
    wg = np.asarray(L0_ffn_w_gate, dtype=np.float32)
    wu = np.asarray(L0_ffn_w_up, dtype=np.float32)
    wd = np.asarray(L0_ffn_w_down, dtype=np.float32)
    shared = {
        "wqT": np.ascontiguousarray(wq.T),
        "wkT": np.ascontiguousarray(wk.T),
        "wvT": np.ascontiguousarray(wv.T),
        "bq": np.ascontiguousarray(np.asarray(L0_attn_bq, dtype=np.float32)),
        "bk": np.ascontiguousarray(np.asarray(L0_attn_bk, dtype=np.float32)),
        "bv": np.ascontiguousarray(np.asarray(L0_attn_bv, dtype=np.float32)),
        "wgT": np.ascontiguousarray(wg.T),
        "wuT": np.ascontiguousarray(wu.T),
        "wdT": np.ascontiguousarray(wd.T),
    }
    in_maps = []
    for core in range(8):
        b, h = core // 2, core % 2
        xb = x[b]
        xcore = np.concatenate([xb[h * TOWN:(h + 1) * TOWN],
                                xb[(1 - h) * TOWN:(2 - h) * TOWN]], axis=0)
        m = {"x": np.ascontiguousarray(xcore)}
        m.update(shared)
        in_maps.append(m)

    if _NC_CACHE is None:
        _NC_CACHE = build_core()
    nc = _NC_CACHE
    res = run_bass_kernel_spmd(nc, in_maps, core_ids=list(range(8)))
    out = np.empty((4, T, D), dtype=np.float32)
    for core in range(8):
        b, h = core // 2, core % 2
        out[b, h * TOWN:(h + 1) * TOWN] = res.results[core]["out"]
    return out


# version 17 (final)
# speedup vs baseline: 1.1108x; 1.0621x over previous
import numpy as np
import concourse.bass as bass
import concourse.mybir as mybir
from concourse.bass_utils import run_bass_kernel_spmd

F32 = mybir.dt.float32
BF16 = mybir.dt.bfloat16
AF = mybir.ActivationFunctionType
ALU = mybir.AluOpType
AX = mybir.AxisListType

P = 128
T = 4096
TOWN = 2048
D = 1024
INNER = 4096
NT = T // P
NOWN = TOWN // P
CD = D // P
NKC = T // P
RC = 12582912.0
DEBUG = False

ENGINES = ("sync", "gpsimd", "tensor", "vector", "scalar")


class Sem:
    __slots__ = ("name", "n", "handle")

    def __init__(self, name):
        self.name, self.n, self.handle = name, 0, None


class Bld:
    def __init__(self):
        self.streams = {e: [] for e in ENGINES}
        self.sems = []

    def sem(self, name):
        s = Sem(name)
        self.sems.append(s)
        return s

    def emit(self, eng, fn, waits=(), inc=None, step=1, drain=None):
        ws = tuple((s, v) for (s, v) in waits if v > 0)
        if drain is None:
            drain = eng in ("vector", "scalar")
        if drain:
            self.streams[eng].append((fn, ws, None, 1))
            self.streams[eng].append((lambda e: e.drain(), (), inc, step))
        else:
            self.streams[eng].append((fn, ws, inc, step))
        if inc is not None:
            inc.n += step
            return inc.n
        return None

    def dma(self, eng, out, in_, waits=(), inc=None, **kw):
        return self.emit(eng, lambda e: e.dma_start(out=out, in_=in_, **kw),
                         waits, inc, 16)



def build_core():
    nc = bass.Bass("TRN2", target_bir_lowering=False)
    bd = Bld()
    sSY = bd.sem("sSY")
    sGP = bd.sem("sGP")
    sGC = bd.sem("sGC")
    sTE = bd.sem("sTE")
    sVE = bd.sem("sVE")
    sSC = bd.sem("sSC")
    sP = [bd.sem(f"sP{i}") for i in range(8)]
    sW = [bd.sem(f"sW{i}") for i in range(4)]
    sWS = [bd.sem("sWS0"), bd.sem("sWS1")]
    sAT = bd.sem("sAT")
    sWB = [bd.sem("sWB0"), bd.sem("sWB1")]
    sHI = [bd.sem(f"sHI{i}") for i in range(4)]
    sXA = [bd.sem(f"sXA{i}") for i in range(4)]
    sXN = [bd.sem(f"sXN{i}") for i in range(4)]
    sXQ = [bd.sem(f"sXQ{i}") for i in range(2)]
    sVH = bd.sem("sVH")
    sQH = bd.sem("sQH")
    sVL = [bd.sem(f"sVL{i}") for i in range(2)]
    sQL = bd.sem("sQL")
    sXI = bd.sem("sXI")
    sXT = bd.sem("sXT")
    sWG = bd.sem("sWG")
    sWU = bd.sem("sWU")
    sWD = [bd.sem(f"sWD{i}") for i in range(4)]
    sHF = bd.sem("sHF")
    sEB = [bd.sem(f"sEB{i}") for i in range(4)]
    E = bd.emit
    DMA = bd.dma

    x_h = nc.dram_tensor("x", [T, D], F32, kind="ExternalInput").ap()
    wqT_h = nc.dram_tensor("wqT", [D, D], F32, kind="ExternalInput").ap()
    wkT_h = nc.dram_tensor("wkT", [D, D], F32, kind="ExternalInput").ap()
    wvT_h = nc.dram_tensor("wvT", [D, D], F32, kind="ExternalInput").ap()
    bq_h = nc.dram_tensor("bq", [D], F32, kind="ExternalInput").ap()
    bk_h = nc.dram_tensor("bk", [D], F32, kind="ExternalInput").ap()
    bv_h = nc.dram_tensor("bv", [D], F32, kind="ExternalInput").ap()
    wgT_h = nc.dram_tensor("wgT", [D, INNER], F32, kind="ExternalInput").ap()
    wuT_h = nc.dram_tensor("wuT", [D, INNER], F32, kind="ExternalInput").ap()
    wdT_h = nc.dram_tensor("wdT", [INNER, D], F32, kind="ExternalInput").ap()
    out_h = nc.dram_tensor("out", [TOWN, D], F32, kind="ExternalOutput").ap()

    kd = "ExternalOutput" if DEBUG else "Internal"
    xn_h = nc.dram_tensor("xn_d", [T, D], BF16, kind=kd).ap()
    xi_h = nc.dram_tensor("xi_d", [TOWN, D], BF16, kind=kd).ap()
    v_h = nc.dram_tensor("v_d", [T, D], BF16, kind=kd).ap()
    q_h = nc.dram_tensor("q_d", [D, TOWN], BF16, kind=kd).ap()
    at_h = nc.dram_tensor("at_d", [TOWN, D], BF16, kind=kd).ap()
    hi_h = nc.dram_tensor("hi_d", [TOWN, INNER], BF16, kind=kd).ap()
    wdq_h = nc.dram_tensor("wdq_d", [INNER, D], BF16, kind="Internal").ap()
    wgq_h = nc.dram_tensor("wgq_d", [D, INNER], BF16, kind=kd).ap()
    wuq_h = nc.dram_tensor("wuq_d", [D, INNER], BF16, kind=kd).ap()
    if DEBUG:
        k_dbg = nc.dram_tensor("k_dbg", [D, T], BF16, kind="ExternalOutput").ap()
        s_dbg = nc.dram_tensor("s_dbg", [P, 9 * NT], F32, kind="ExternalOutput").ap()
        pd_dbg = nc.dram_tensor("pd_dbg", [P, 96], F32, kind="ExternalOutput").ap()

    AR_K = nc.alloc_sbuf_tensor("AR_K", [P, 8 * 4096], BF16)
    AR_PT = nc.alloc_sbuf_tensor("AR_PT", [P, 8 * 4096], BF16)
    AR_W = nc.alloc_sbuf_tensor("AR_W", [P, 8 * 3072], BF16)
    MSC8 = nc.alloc_sbuf_tensor("MSC8", [P, 4096], BF16)
    STR8 = nc.alloc_sbuf_tensor("STR8", [P, 4096], BF16)
    P1B = nc.alloc_sbuf_tensor("P1B", [P, 4096], BF16)

    KF = AR_K.reshape([P, CD, T])
    WG = AR_K.reshape([P, CD, INNER])
    HFM = AR_K.reshape([P, NKC, 1024])
    A32 = AR_PT.bitcast(F32)
    XNQ = AR_PT.reshape([P, 32, 1024])
    PT = AR_PT.reshape([P, NKC, 1024])
    WU = AR_PT.reshape([P, CD, INNER])
    WD = AR_PT.reshape([P, NKC, 1024])
    WQKV = AR_W.reshape([P, 3, CD, 1024])
    AWB = AR_W
    AWF = AR_W.bitcast(F32)
    V48 = AR_W.reshape([P, 48, 512])
    Q24 = AR_W.reshape([P, 24, 1024])
    XI12 = AR_W.reshape([P, 12, 2048])
    M8F = MSC8.bitcast(F32)
    S8F = STR8.bitcast(F32)
    P8F = P1B.bitcast(F32)

    def sm(name, shape, dt=F32):
        return nc.alloc_sbuf_tensor(name, shape, dt)

    s1_all = sm("s1_all", [P, NT])
    s2_all = sm("s2_all", [P, NT])
    m_all = sm("m_all", [P, NT])
    vp_all = sm("vp_all", [P, NT])
    t1_all = sm("t1_all", [P, NT])
    svp_all = sm("svp_all", [P, NT])
    r_all = sm("r_all", [P, NT])
    bact_all = sm("bact_all", [P, NT])
    ax_all = sm("ax_all", [P, NOWN])
    t_all = sm("t_all", [P, NOWN])
    sx_all = sm("sx_all", [P, NOWN])
    isx_all = sm("isx_all", [P, NOWN])
    gscale_all = sm("gscale_all", [P, NOWN])
    thr_all = sm("thr_all", [P, NOWN])
    dscale_all = sm("dscale_all", [P, NOWN])
    bk_cols = sm("bk_cols", [P, CD])
    bq_cols = sm("bq_cols", [P, CD])
    bq32_cols = sm("bq32_cols", [P, CD])
    bv_bf = sm("bv_bf", [1, D], BF16)
    onesr = sm("onesr", [1, P], BF16)
    negr = sm("negr", [1, P], BF16)
    onesc = sm("onesc", [P, 1], BF16)
    onescf = sm("onescf", [P, 1], F32)
    ones_rf = sm("ones_rf", [1, P], F32)
    ones11f = sm("ones11f", [1, 1], F32)
    c127 = sm("c127", [P, 1], F32)
    kmax11 = sm("kmax11", [1, 1], F32)
    tmax11 = sm("tmax11", [1, 1], F32)
    u_col = sm("u_col", [P, 1], F32)
    i0_col = sm("i0_col", [P, 1], F32)
    e1_col = sm("e1_col", [P, 1], F32)
    mk2 = sm("mk2", [1, 1], F32)
    wsum = sm("wsum", [P, 3], F32)
    parts_all = sm("parts_all", [P, 96], F32)
    part_col = sm("part_col", [P, 1], F32)
    cg_col = sm("cg_col", [P, 3], F32)
    sg_col = sm("sg_col", [P, 3], F32)
    cc_col = sm("cc_col", [P, 1], F32)
    linv_cols = sm("linv_cols", [P, 8], F32)
    hm_col = sm("hm_col", [P, 1], F32)
    tc_col = sm("tc_col", [P, 1], F32)
    hs_col = sm("hs_col", [P, 1], F32)
    d1_col = sm("d1_col", [P, 1], F32)
    cgv = sm("cgv", [1, 3], F32)

    PS = [nc.alloc_psum_tensor(f"ps{i}", [P, 1024], F32) for i in range(4)]

    E("vector", lambda e: e.memset(onesr[:], 1.0))
    E("vector", lambda e: e.memset(negr[:], -1.0))
    E("vector", lambda e: e.memset(onesc[:], 1.0))
    E("vector", lambda e: e.memset(onescf[:], 1.0))
    E("vector", lambda e: e.memset(ones_rf[:], 1.0))
    E("vector", lambda e: e.memset(ones11f[:], 1.0))
    E("vector", lambda e: e.memset(c127[:], 127.0))
    consts_done = E("vector", lambda e: e.memset(wsum[:], 0.0), inc=sVE)

    for wi, wT in ((0, wqT_h), (1, wkT_h), (2, wvT_h)):
        DMA("gpsimd", WQKV[:, wi, :, :],
            wT.rearrange("(c p) d -> p c d", p=P), inc=sWB[0])
    wqkv_done = (sWB[0], sWB[0].n)
    DMA("gpsimd", bk_cols[:], bk_h.rearrange("(c p) -> p c", p=P), inc=sWB[1],
        allow_slow_non_contiguous=True)
    DMA("gpsimd", bq_cols[:], bq_h.rearrange("(c p) -> p c", p=P), inc=sWB[1],
        allow_slow_non_contiguous=True)
    DMA("gpsimd", bv_bf[:], bv_h.rearrange("(o d) -> o d", o=1), inc=sWB[1])
    bias_done = (sWB[1], sWB[1].n)
    bq32_done = E("vector", lambda e: e.tensor_scalar(
        bq32_cols[:], bq_cols[:], 1.0 / 32.0, None, ALU.mult),
        waits=[bias_done], inc=sVE)

    strip_srcs = []
    for c in range(CD):
        for s in range(4):
            strip_srcs.append(wgT_h[c * P:(c + 1) * P, s * 1024:(s + 1) * 1024])
    for c in range(CD):
        for s in range(4):
            strip_srcs.append(wuT_h[c * P:(c + 1) * P, s * 1024:(s + 1) * 1024])
    for ic in range(NKC):
        strip_srcs.append(wdT_h[ic * P:(ic + 1) * P, :])
    NS = len(strip_srcs)
    p1_bufs = [P8F[:, 0:1024], P8F[:, 1024:2048]]
    p1_dma_cnt = [None] * NS
    p1_vec_cnt = [None] * NS
    p1 = {"dma": 0, "vec": 0}

    def p1_emit_dma():
        si = p1["dma"]
        if si >= NS:
            return
        w = []
        if si >= 2:
            assert p1_vec_cnt[si - 2] is not None
            w.append((sVE, p1_vec_cnt[si - 2]))
        p1_dma_cnt[si] = (sP[si % 8], DMA(
            "gpsimd", p1_bufs[si % 2], strip_srcs[si], waits=w,
            inc=sP[si % 8]))
        p1["dma"] += 1

    def p1_emit_vec():
        si = p1["vec"]
        if si >= NS or p1_dma_cnt[si] is None:
            return
        p1_vec_cnt[si] = E("vector", lambda e, b=p1_bufs[si % 2], si=si:
                           e.tensor_reduce(
            parts_all[:, si:si + 1], b, AX.X, ALU.add,
            apply_absolute_value=True),
          waits=[p1_dma_cnt[si]], inc=sVE)
        p1["vec"] += 1

    def p1_tick(n=1):
        for _ in range(n):
            p1_emit_vec()
            p1_emit_dma()

    p1_emit_dma()
    p1_emit_dma()


    xA4 = [A32[:, 8192:12288], A32[:, 12288:16384]]
    junkA = M8F[:, 1024:2048]
    xnb4 = [MSC8[:, 0:4096], None]
    xbq = [A32[:, 8192:9216], A32[:, 9216:10240]]
    xnf = [A32[:, 10240:11264], A32[:, 11264:12288]]
    vppad = A32[:, 12288:12800]
    lnpadA = A32[:, 12800:13312]
    rpadA = A32[:, 13312:13824]
    xib2 = [AR_PT[:, 28672:29696], AR_PT[:, 29696:30720]]
    xnb = [MSC8[:, 0:1024], MSC8[:, 1024:2048]]
    xib = [MSC8[:, 2048:3072], MSC8[:, 3072:4096]]

    cons = {0: [], 1: []}
    for k in range(NT // 4):
        par = k % 2
        ldq_eng, ldq_sem = ("sync", sXA[k % 4])
        ld = DMA(ldq_eng, xA4[par].rearrange("p (t d) -> p t d", t=4),
                 x_h[k * 4 * P:(k + 1) * 4 * P, :].rearrange(
                     "(t p) d -> p t d", p=P),
                 waits=cons[par], inc=ldq_sem)
        last_sc = last_vc = None
        for j in range(4):
            i = 4 * k + j
            last_sc = E("scalar", lambda e, p=par, j=j, i=i: e.activation(
                junkA, xA4[p][:, j * 1024:(j + 1) * 1024], AF.Square,
                accum_out=s2_all[:, i:i + 1]),
                waits=([(ldq_sem, ld)] if j == 0 else []), inc=sSC)
            last_vc = E("vector", lambda e, p=par, j=j, i=i: e.reduce_sum(
                s1_all[:, i:i + 1], xA4[p][:, j * 1024:(j + 1) * 1024],
                AX.X), waits=([(ldq_sem, ld)] if j == 0 else []), inc=sVE)
        cons[par] = [(sSC, last_sc), (sVE, last_vc)]
    a1_sc = sSC.n
    a1_sc = sSC.n

    E("vector", lambda e: e.tensor_scalar(
        m_all[:], s1_all[:], 1.0 / D, None, ALU.mult),
      waits=[(sSC, a1_sc)])
    E("vector", lambda e: e.tensor_scalar(
        vp_all[:], s2_all[:], 1.0 / D, None, ALU.mult))
    E("vector", lambda e: e.tensor_copy(svp_all[:], m_all[:]))
    E("vector", lambda e: e.tensor_mul(t1_all[:], m_all[:], svp_all[:]))
    E("vector", lambda e: e.tensor_sub(vp_all[:], vp_all[:], t1_all[:]))
    E("vector", lambda e: e.tensor_scalar(
        vp_all[:], vp_all[:], 1e-5, None, ALU.add))
    E("vector", lambda e: e.memset(vppad[:], 1.0))
    a2v1 = E("vector", lambda e: e.tensor_copy(vppad[:, 0:NT], vp_all[:]),
             inc=sVE)
    E("scalar", lambda e: e.activation(lnpadA[:], vppad[:], AF.Ln),
      waits=[(sVE, a2v1)])
    a2s = E("scalar", lambda e: e.activation(rpadA[:], lnpadA[:], AF.Exp,
                                             scale=-0.5), inc=sSC)
    E("vector", lambda e: e.tensor_copy(r_all[:], rpadA[:, 0:NT]),
      waits=[(sSC, a2s)])
    E("vector", lambda e: e.tensor_mul(t1_all[:], vp_all[:], r_all[:]))
    E("vector", lambda e: e.tensor_mul(t1_all[:], t1_all[:], r_all[:]))
    E("vector", lambda e: e.tensor_scalar(
        t1_all[:], t1_all[:], -0.5, 1.5, ALU.mult, ALU.add))
    E("vector", lambda e: e.tensor_mul(r_all[:], r_all[:], t1_all[:]))
    E("vector", lambda e: e.tensor_mul(t1_all[:], m_all[:], r_all[:]))
    a2_done = E("vector", lambda e: e.tensor_scalar(
        bact_all[:], t1_all[:], -1.0, None, ALU.mult), inc=sVE)
    if DEBUG:
        DMA("sync", s_dbg[:, 7 * NT:8 * NT], r_all[:],
            waits=[(sVE, a2_done)], inc=sSY)
        DMA("gpsimd", s_dbg[:, 0:NT], s1_all[:], waits=[(sVE, a2_done)], inc=sGP)
        DMA("gpsimd", s_dbg[:, NT:2 * NT], s2_all[:], inc=sGP)
        DMA("gpsimd", s_dbg[:, 2 * NT:3 * NT], r_all[:], inc=sGP)
        DMA("gpsimd", s_dbg[:, 3 * NT:4 * NT], bact_all[:], inc=sGP)
        DMA("gpsimd", s_dbg[:, 4 * NT:5 * NT], vp_all[:], inc=sGP)
        DMA("gpsimd", s_dbg[:, 5 * NT:6 * NT], svp_all[:], inc=sGP)

    xn_chunk_store = {}
    xnb4_free = None
    for k in range(NT // 4):
        par = k % 2
        ldq_eng, ldq_sem = ("sync", sXA[k % 4])
        wldk = list(cons[par])
        if k < 2:
            wldk.append((sVE, a2_done))
        ld = DMA(ldq_eng, xA4[par].rearrange("p (t d) -> p t d", t=4),
                 x_h[k * 4 * P:(k + 1) * 4 * P, :].rearrange(
                     "(t p) d -> p t d", p=P),
                 waits=wldk, inc=ldq_sem)
        last_sc = None
        for j in range(4):
            i = 4 * k + j
            w2 = [(sVE, a2_done)]
            if j == 0:
                w2.append((ldq_sem, ld))
                if xnb4_free:
                    w2.append(xnb4_free)
            last_sc = E("scalar", lambda e, p=par, j=j, i=i: e.activation(
                xnb4[0][:, j * 1024:(j + 1) * 1024],
                xA4[p][:, j * 1024:(j + 1) * 1024], AF.Identity,
                bias=bact_all[:, i:i + 1], scale=r_all[:, i:i + 1]),
                waits=w2, inc=sSC, drain=(j == 3))
        DMA("sync",
            xn_h[k * 4 * P:(k + 1) * 4 * P, :].rearrange(
                "(t p) d -> p t d", p=P),
            xnb4[0].rearrange("p (t d) -> p t d", t=4),
            waits=[(sSC, last_sc)], inc=sXN[k % 4])
        xnb4_free = (sXN[k % 4], sXN[k % 4].n)
        xn_chunk_store[k] = xnb4_free
        cons[par] = [(sSC, last_sc)]
    a3_last_store = xnb4_free

    vt_bf = [STR8[:, 0:1024], STR8[:, 1024:2048]]
    qt_bf = [STR8[:, 2048:3072], STR8[:, 3072:4096]]
    evacK = {0: 0, 1: 0}
    evacV = {0: 0, 1: 0}
    vt_free = {0: 0, 1: 0}
    qt_free = {0: 0, 1: 0}
    xnq_mm_done = {0: 0, 1: 0}

    def emit_xnq(qr):
        qp = qr % 2
        tq = None
        for c in range(CD):
            w = []
            if c == 0:
                w = [xn_chunk_store[2 * qr], xn_chunk_store[2 * qr + 1],
                     (sTE, xnq_mm_done[qp]),
                     (sSC, sSC.n)]
            tq = DMA("sync", XNQ[:, qp * 8 + c, :],
                     xn_h[qr * 1024:(qr + 1) * 1024, c * P:(c + 1) * P],
                     waits=w, inc=sXQ[qr % 2], transpose=True)
        return (sXQ[qr % 2], tq)

    xq_free = {}
    xib2_free = {}
    def a3b_emit(i):
        par = i % 2
        wld = list(xq_free.get(par, []))
        if i < 2:
            wld += cons[0] + cons[1]
        ldq = DMA("gpsimd", xbq[par], x_h[i * P:(i + 1) * P, :],
                  waits=wld, inc=sEB[i % 4])
        wsq2 = [(sEB[i % 4], ldq), (sVE, a2_done)] + list(xq_free.get(par, []))
        scq = E("scalar", lambda e, p=par, i=i: e.activation(
            xnf[p], xbq[p], AF.Identity, bias=bact_all[:, i:i + 1],
            scale=r_all[:, i:i + 1]), waits=wsq2,
            inc=sSC, drain=False)
        E("vector", lambda e, p=par, i=i: e.reduce_max(
            ax_all[:, i:i + 1], xnf[p], AX.X, apply_absolute_value=True),
          waits=[(sSC, scq)])
        E("vector", lambda e, i=i: e.tensor_scalar(
            t_all[:, i:i + 1], ax_all[:, i:i + 1], 1e-5, None, ALU.max))
        E("vector", lambda e, i=i: e.tensor_scalar(
            isx_all[:, i:i + 1], t_all[:, i:i + 1], 1.0 / 127.0,
            None, ALU.mult))
        vq = E("vector", lambda e, i=i: e.tensor_copy(
            vppad[:, 0:1], t_all[:, i:i + 1]), inc=sVE)
        E("scalar", lambda e: e.activation(lnpadA[:], vppad[:], AF.Ln),
          waits=[(sVE, vq)])
        scx = E("scalar", lambda e: e.activation(
            rpadA[:], lnpadA[:], AF.Exp, scale=-1.0), inc=sSC)
        E("vector", lambda e, i=i: e.tensor_tensor(
            e1_col[:], t_all[:, i:i + 1], rpadA[:, 0:1], ALU.mult),
          waits=[(sSC, scx)])
        E("vector", lambda e: e.tensor_scalar(
            e1_col[:], e1_col[:], -1.0, 2.0, ALU.mult, ALU.add))
        E("vector", lambda e: e.tensor_tensor(
            e1_col[:], rpadA[:, 0:1], e1_col[:], ALU.mult))
        sxv = E("vector", lambda e, i=i: e.tensor_scalar(
            sx_all[:, i:i + 1], e1_col[:], 127.0, None, ALU.mult), inc=sVE)
        E("vector", lambda e, p=par, i=i: e.tensor_tensor(
            xnf[p], xnf[p], sx_all[:, i:i + 1].broadcast_to((P, 1024)),
            ALU.mult), waits=[(sVE, sxv)], drain=False)
        wib = [xib2_free[par]] if par in xib2_free else []
        xiv = E("vector", lambda e, p=par: e.tensor_scalar(
            xib2[p], xnf[p], RC, RC, ALU.add, ALU.subtract),
            waits=wib, inc=sVE, drain=False)
        xq_free[par] = [(sVE, xiv)]
        DMA("sync", xi_h[i * P:(i + 1) * P, :], xib2[par],
            waits=[(sVE, xiv)], inc=sXI)
        xib2_free[par] = (sXI, sXI.n)

    xnq_ready = {0: emit_xnq(0), 1: emit_xnq(1)}
    for qr in range(4):
        qp = qr % 2
        xnq_done = xnq_ready[qr]

        for o in range(CD):
            par = o % 2
            ps = PS[par]
            for c in range(CD):
                w = []
                if c == 0:
                    w = [xnq_done, wqkv_done, (sSC, evacK[par])]
                lhsT = WQKV[:, 1, c, o * P:(o + 1) * P]
                if True:
                    pass
                E("tensor", lambda e, l=lhsT, p=ps, c=c, qp=qp:
                  e.matmul(p[:, 0:512], l, XNQ[:, qp * 8 + c, 0:512],
                           start=(c == 0), stop=(c == CD - 1),
                           skip_group_check=True), waits=w)
                E("tensor", lambda e, l=lhsT, p=ps, c=c, qp=qp:
                  e.matmul(p[:, 512:1024], l, XNQ[:, qp * 8 + c, 512:1024],
                           start=(c == 0), stop=(c == CD - 1),
                           skip_group_check=True),
                  inc=(sTE if c == CD - 1 else None))
            grp = sTE.n
            evacK[par] = E("scalar", lambda e, p=ps, o=o, qr=qr: e.activation(
                KF[:, o, qr * 1024:(qr + 1) * 1024], p[:], AF.Identity,
                bias=bk_cols[:, o:o + 1]),
                waits=[(sTE, grp), bias_done], inc=sSC, drain=False)

        if qr == 3:
            k4_te = sTE.n
            k4_sc = sSC.n
        for tb in range(CD):
            par = tb % 2
            ps = PS[2 + par]
            for c in range(CD):
                w = []
                if c == 0:
                    w = [xnq_done, bias_done, (sSC, evacV[par]),
                         (sVE, consts_done)]
                lhsT = XNQ[:, qp * 8 + c, tb * P:(tb + 1) * P]
                E("tensor", lambda e, l=lhsT, p=ps, c=c:
                  e.matmul(p[:, 0:512], l, WQKV[:, 2, c, 0:512],
                           start=(c == 0), stop=False,
                           skip_group_check=True), waits=w)
                E("tensor", lambda e, l=lhsT, p=ps, c=c:
                  e.matmul(p[:, 512:1024], l, WQKV[:, 2, c, 512:1024],
                           start=(c == 0), stop=False,
                           skip_group_check=True))
            E("tensor", lambda e, p=ps: e.matmul(
                p[:, 0:512], onesr[0:1, :], bv_bf[0:1, 0:512],
                start=False, stop=True, skip_group_check=True))
            E("tensor", lambda e, p=ps: e.matmul(
                p[:, 512:1024], onesr[0:1, :], bv_bf[0:1, 512:1024],
                start=False, stop=True, skip_group_check=True), inc=sTE)
            grp = sTE.n
            evacV[par] = E("scalar", lambda e, p=ps, par=par: e.copy(
                vt_bf[par], p[:]), waits=[(sTE, grp), (sVH, vt_free[par])],
                inc=sSC, drain=False)
            vt_free[par] = DMA(
                "sync", v_h[qr * 1024 + tb * P: qr * 1024 + (tb + 1) * P, :],
                vt_bf[par], waits=[(sSC, evacV[par])], inc=sVH)

        if qr < 2:
            for o in range(CD):
                par = o % 2
                ps = PS[par]
                for c in range(CD):
                    w = []
                    if c == 0:
                        w = [xnq_done, (sSC, evacK[par]),
                             (sVE, bq32_done)]
                    lhsT = WQKV[:, 0, c, o * P:(o + 1) * P]
                    E("tensor", lambda e, l=lhsT, p=ps, c=c, qp=qp:
                      e.matmul(p[:, 0:512], l, XNQ[:, qp * 8 + c, 0:512],
                               start=(c == 0), stop=(c == CD - 1),
                               skip_group_check=True), waits=w)
                    E("tensor", lambda e, l=lhsT, p=ps, c=c, qp=qp:
                      e.matmul(p[:, 512:1024], l,
                               XNQ[:, qp * 8 + c, 512:1024],
                               start=(c == 0), stop=(c == CD - 1),
                               skip_group_check=True),
                      inc=(sTE if c == CD - 1 else None))
                grp = sTE.n
                evacK[par] = E("scalar", lambda e, p=ps, o=o, par=par:
                               e.activation(qt_bf[par], p[:], AF.Identity,
                                            bias=bq32_cols[:, o:o + 1],
                                            scale=1.0 / 32.0),
                               waits=[(sTE, grp), (sQH, qt_free[par])],
                               inc=sSC, drain=False)
                qt_free[par] = DMA(
                    "sync",
                    q_h[o * P:(o + 1) * P, qr * 1024:(qr + 1) * 1024],
                    qt_bf[par], waits=[(sSC, evacK[par])], inc=sQH)
        xnq_mm_done[qp] = sTE.n
        if qr + 2 < 4:
            xnq_ready[qr + 2] = emit_xnq(qr + 2)

        for _ in range(26):
            p1_emit_vec()
            p1_emit_dma()
    while p1["vec"] < NS:
        p1_emit_vec()
        p1_emit_dma()
    p1_done = sVE.n
    for _i in range(NOWN):
        a3b_emit(_i)
    a3b_done = sVE.n
    a3b_xi = (sXI, sXI.n)
    b_te = sTE.n
    b_sc = sSC.n
    vh_total = (sVH, sVH.n)
    qh_total = (sQH, sQH.n)


    if DEBUG:
        for c in range(CD):
            DMA("gpsimd", k_dbg[c * P:(c + 1) * P, :], KF[:, c, :],
                waits=[(sSC, b_sc)], inc=sGP)

    accb = AWB[:, 0:4096]
    tmpb = AWB[:, 4096:8192]
    knr = AWF[:, 4096:8192]
    ks0 = E("scalar", lambda e: e.activation(accb, KF[:, 0, :], AF.Square),
            waits=[(sSC, k4_sc), (sTE, k4_te)], inc=sSC, drain=False)
    kn_v = None
    for c in range(1, CD):
        ksc = E("scalar", lambda e, c=c: e.activation(
            tmpb, KF[:, c, :], AF.Square),
            waits=([(sVE, kn_v)] if kn_v else []), inc=sSC, drain=False)
        kn_v = E("vector", lambda e: e.tensor_add(accb, accb, tmpb),
                 waits=[(sSC, ksc)], inc=sVE)
    km_init = E("vector", lambda e: e.memset(kmax11[:], 0.0), inc=sVE)
    kn_prev = km_init
    for j in range(8):
        mmj = E("tensor", lambda e, j=j: e.matmul(
            PS[0][0:1, 0:512], onesc[:, 0:1], accb[:, j * 512:(j + 1) * 512],
            start=True, stop=True, skip_group_check=True),
            waits=[(sVE, kn_v if j == 0 else kn_prev),
                   (sSC, b_sc)], inc=sTE)
        E("vector", lambda e: e.reduce_max(
            tmax11[:], PS[0][0:1, 0:512], AX.X), waits=[(sTE, mmj)])
        kn_prev = E("vector", lambda e: e.tensor_max(
            kmax11[:], kmax11[:], tmax11[:]), inc=sVE)
    mk2_done = E("vector", lambda e: e.tensor_scalar(
        mk2[:], kmax11[:], 1.01, None, ALU.mult), inc=sVE)

    for w in range(3):
        E("vector", lambda e, w=w: e.reduce_sum(
            wsum[:, w:w + 1], parts_all[:, w * 32:(w + 1) * 32], AX.X),
          waits=([(sVE, p1_done)] if w == 0 else []))
    E("vector", lambda e: e.engine_nop(), inc=sVE)
    wsum_done = sVE.n
    cg_done = {}
    cv_prev = 0
    csc_prev = 0
    for w in range(3):
        mmn = E("tensor", lambda e, w=w: e.matmul(
            PS[0][0:1, 0:1], wsum[:, w:w + 1], onescf[:, 0:1],
            start=True, stop=True, skip_group_check=True),
            waits=([(sVE, wsum_done), (sVE, kn_prev)]
                   if w == 0 else [(sVE, cv_prev)]), inc=sTE)
        cv = E("vector", lambda e, w=w: e.tensor_scalar(
            cgv[0:1, w:w + 1], PS[0][0:1, 0:1],
            1.0 / (D * INNER), 1e-5, ALU.mult, ALU.max),
            waits=[(sTE, mmn)], inc=sVE)
        cv_prev = cv
        mmb = E("tensor", lambda e, w=w: e.matmul(
            PS[1][:, 0:1], ones_rf[0:1, :], cgv[0:1, w:w + 1],
            start=True, stop=True, skip_group_check=True),
            waits=[(sVE, cv), (sVE, csc_prev)], inc=sTE)
        csc = E("vector", lambda e, w=w: e.tensor_copy(
            cg_col[:, w:w + 1], PS[1][:, 0:1]), waits=[(sTE, mmb)])
        csc2 = E("vector", lambda e, w=w: e.tensor_copy(
            vppad[:, 0:1], cg_col[:, w:w + 1]), inc=sVE)
        csc_prev = csc2
        E("scalar", lambda e: e.activation(
            lnpadA[:], vppad[:], AF.Ln), waits=[(sVE, csc2)])
        sgx = E("scalar", lambda e: e.activation(
            rpadA[:], lnpadA[:], AF.Exp, scale=-1.0), inc=sSC)
        E("vector", lambda e, w=w: e.tensor_tensor(
            e1_col[:], cg_col[:, w:w + 1], rpadA[:, 0:1], ALU.mult),
          waits=[(sSC, sgx)])
        E("vector", lambda e: e.tensor_scalar(
            e1_col[:], e1_col[:], -1.0, 2.0, ALU.mult, ALU.add))
        cg_done[w] = E("vector", lambda e, w=w: e.tensor_tensor(
            sg_col[:, w:w + 1], rpadA[:, 0:1], e1_col[:], ALU.mult), inc=sVE)
    E("vector", lambda e: e.tensor_tensor(
        gscale_all[:], isx_all[:], cg_col[:, 0:1].broadcast_to((P, NOWN)),
        ALU.mult), waits=[(sVE, cg_done[2]), (sVE, a3b_done)])
    E("vector", lambda e: e.tensor_tensor(
        thr_all[:], sx_all[:], sg_col[:, 1:2].broadcast_to((P, NOWN)),
        ALU.mult))
    E("vector", lambda e: e.tensor_scalar(
        thr_all[:], thr_all[:], 1e-5, None, ALU.mult))
    E("vector", lambda e: e.tensor_tensor(
        cc_col[:], cg_col[:, 1:2], cg_col[:, 2:3], ALU.mult))
    eprol = E("vector", lambda e: e.tensor_scalar(
        cc_col[:], cc_col[:], 1.0 / 127.0, None, ALU.mult), inc=sVE)
    eprol_sc = sSC.n
    if DEBUG:
        DMA("gpsimd", pd_dbg[:], parts_all[:], waits=[(sVE, eprol)], inc=sGP)
        DMA("gpsimd", s_dbg[:, 8 * NT:8 * NT + 3], wsum[:],
            waits=[(sVE, eprol)], inc=sGP)
        DMA("gpsimd", s_dbg[:, 8 * NT + 3:8 * NT + 6], sg_col[:],
            waits=[(sVE, eprol)], inc=sGP)

    wq_srcs = ([("g", c, s, wgT_h) for c in range(CD) for s in range(4)] +
               [("u", c, s, wuT_h) for c in range(CD) for s in range(4)] +
               [("d", ic, None, wdT_h) for ic in range(NKC)])
    wgq_h2 = {"g": wgq_h, "u": wuq_h, "d": wdq_h}
    wq_bf = [MSC8[:, 0:1024], MSC8[:, 1024:2048]]
    wq_state = {"i": 0, "q1f": {0: (sVE, p1_done), 1: (sVE, p1_done)},
                "bff": {0: (sWS[0], 0), 1: (sWS[1], 0)}}
    wsg_i = {"g": 0, "u": 1, "d": 2}

    def wq_tick(n=1):
        for _ in range(n):
            i = wq_state["i"]
            if i >= len(wq_srcs):
                return
            w, c, s, srcT = wq_srcs[i]
            par = i % 2
            if s is None:
                srcap = srcT[c * P:(c + 1) * P, :]
                dstap = wgq_h2[w][c * P:(c + 1) * P, :]
            else:
                srcap = srcT[c * P:(c + 1) * P, s * 1024:(s + 1) * 1024]
                dstap = wgq_h2[w][c * P:(c + 1) * P,
                                  s * 1024:(s + 1) * 1024]
            ld = DMA("gpsimd", q1[par], srcap,
                     waits=[wq_state["q1f"][par]], inc=sW[i % 4])
            wi = wsg_i[w]
            E("vector", lambda e, par=par, wi=wi: e.tensor_scalar(
                q1[par], q1[par], sg_col[:, wi:wi + 1], RC,
                ALU.mult, ALU.add),
                waits=[(sW[i % 4], ld), (sVE, cg_done[wi]),
                       wq_state["bff"][par]] +
                      ([a3_last_store] if i < 2 else []),
                drain=False)
            E("vector", lambda e, par=par: e.tensor_scalar(
                q1[par], q1[par], RC, 1.0, ALU.subtract, ALU.min),
              drain=False)
            clip = E("vector", lambda e, par=par: e.tensor_scalar(
                wq_bf[par], q1[par], -1.0, None, ALU.max), inc=sVE)
            wq_state["q1f"][par] = (sVE, clip)
            wq_state["bff"][par] = (sWS[par], DMA(
                "gpsimd", dstap, wq_bf[par], waits=[(sVE, clip)],
                inc=sWS[par]))
            wq_state["i"] = i + 1

    q1 = [P8F[:, 0:1024], P8F[:, 1024:2048]]

    c_row = M8F[0:1, 0:1024]
    c_bf = MSC8[0:1, 2048:3072]
    at_bf = [MSC8[:, 3072:3584], MSC8[:, 3584:4096]]
    u_row = S8F[0:1, 0:1024]
    linv_row = S8F[0:1, 1024:2048]
    accq = AWB[:, 0:1024]
    tmpq = AWB[:, 1024:2048]

    pv_te = b_te
    at_free = {0: 0, 1: 0}
    exp_cnt = {}
    linvc_prev = 0
    for hq in range(2):
        qld = None
        for c in range(CD):
            w = []
            if c == 0:
                w = [qh_total, (sTE, pv_te), (sSC, sSC.n)]
            qld = DMA("sync", Q24[:, 16 + c, :],
                      q_h[c * P:(c + 1) * P, hq * 1024:(hq + 1) * 1024],
                      waits=w, inc=sQL)
        E("scalar", lambda e: e.activation(
            accq, Q24[:, 16, :], AF.Square), waits=[(sQL, qld)],
          drain=False)
        qn_v = None
        for c in range(1, CD):
            qsc = E("scalar", lambda e, c=c: e.activation(
                tmpq, Q24[:, 16 + c, :], AF.Square),
                waits=([(sVE, qn_v)] if qn_v else []), inc=sSC, drain=False)
            qn_v = E("vector", lambda e: e.tensor_add(accq, accq, tmpq),
                     waits=[(sSC, qsc)], inc=sVE)
        qm1 = E("tensor", lambda e: e.matmul(
            PS[0][0:1, 0:512], onesc[:, 0:1], accq[:, 0:512],
            start=True, stop=True, skip_group_check=True),
            waits=[(sVE, qn_v)], inc=sTE)
        qm2 = E("tensor", lambda e: e.matmul(
            PS[0][0:1, 512:1024], onesc[:, 0:1], accq[:, 512:1024],
            start=True, stop=True, skip_group_check=True), inc=sTE)
        yq = E("vector", lambda e: e.tensor_scalar(
            PS[0][0:1, 0:1024], PS[0][0:1, 0:1024], mk2[0:1, 0:1],
            None, ALU.mult),
            waits=[(sTE, qm2), (sVE, mk2_done)], inc=sVE)
        lnq = E("scalar", lambda e: e.activation(
            u_row, PS[0][0:1, 0:1024], AF.Ln), waits=[(sVE, yq)], inc=sSC)
        c_s = E("scalar", lambda e: e.activation(
            c_row, u_row, AF.Exp, scale=0.5), waits=[(sSC, lnq)], inc=sSC)
        cbf_done = E("vector", lambda e: e.tensor_copy(c_bf, c_row),
                     waits=[(sSC, c_s)], inc=sVE)

        for kc in range(NKC):
            par = kc % 2
            ps = PS[par]
            for c in range(CD):
                w = []
                if c == 0:
                    w = [(sVE, cbf_done),
                         (sSC, exp_cnt.get((hq, kc - 2),
                                           exp_cnt.get((hq - 1, NKC - 2 + kc),
                                                       0)))]
                    if hq == 0 and kc == 16:
                        w += [(sVE, a3b_done), a3b_xi]
                lhsT = KF[:, c, kc * P:(kc + 1) * P]
                E("tensor", lambda e, l=lhsT, p=ps, c=c:
                  e.matmul(p[:, 0:512], l, Q24[:, 16 + c, 0:512],
                           start=(c == 0), stop=False,
                           skip_group_check=True), waits=w)
                E("tensor", lambda e, l=lhsT, p=ps, c=c:
                  e.matmul(p[:, 512:1024], l, Q24[:, 16 + c, 512:1024],
                           start=(c == 0), stop=False,
                           skip_group_check=True))
            E("tensor", lambda e, p=ps: e.matmul(
                p[:, 0:512], negr[0:1, :], c_bf[0:1, 0:512],
                start=False, stop=True, skip_group_check=True))
            E("tensor", lambda e, p=ps: e.matmul(
                p[:, 512:1024], negr[0:1, :], c_bf[0:1, 512:1024],
                start=False, stop=True, skip_group_check=True), inc=sTE)
            grp = sTE.n
            exp_cnt[(hq, kc)] = E("scalar", lambda e, p=ps, kc=kc:
                                  e.activation(PT[:, kc, :], p[:], AF.Exp),
                                  waits=[(sTE, grp)], inc=sSC, drain=False)
            wl = [(sSC, exp_cnt[(hq, kc)])]
            if kc == 0:
                wl.append((sVE, linvc_prev))
            E("tensor", lambda e, kc=kc: e.matmul(
                PS[2][0:1, 0:512], onesc[:, 0:1], PT[:, kc, 0:512],
                start=(kc == 0), stop=(kc == NKC - 1),
                skip_group_check=True), waits=wl)
            E("tensor", lambda e, kc=kc: e.matmul(
                PS[2][0:1, 512:1024], onesc[:, 0:1], PT[:, kc, 512:1024],
                start=(kc == 0), stop=(kc == NKC - 1),
                skip_group_check=True),
              inc=(sTE if kc == NKC - 1 else None))
            wq_tick(1)
        l_done = sTE.n
        scores_te_mark = l_done
        lnl = E("scalar", lambda e: e.activation(
            u_row, PS[2][0:1, 0:1024], AF.Ln),
            waits=[(sTE, l_done)], inc=sSC)
        lv = E("scalar", lambda e: e.activation(
            linv_row, u_row, AF.Exp, scale=-1.0),
            waits=[(sSC, lnl)], inc=sSC)
        for qt in range(8):
            E("tensor", lambda e, qt=qt: e.matmul(
                PS[2][:, qt:qt + 1], linv_row[0:1, qt * P:(qt + 1) * P],
                ones11f[0:1, 0:1], start=True, stop=True,
                skip_group_check=True),
              waits=([(sSC, lv)] if qt == 0 else []),
              inc=(sTE if qt == 7 else None))
        linvc = E("vector", lambda e: e.tensor_copy(
            linv_cols[:], PS[2][:, 0:8]), waits=[(sTE, sTE.n)], inc=sVE)
        linvc_prev = linvc

        for dh in range(2):
            vld = None
            varea_guard = sTE.n
            for g in range(8):
                w = []
                if g == 0:
                    w = [(sTE, varea_guard), vh_total]
                vld = DMA(
                    "sync", V48[:, g * 4:(g + 1) * 4, :],
                    v_h.rearrange("(k p) d -> p k d", p=P)[
                        :, g * 4:(g + 1) * 4, dh * 512:(dh + 1) * 512],
                    waits=w, inc=sVL[(hq * 2 + dh) % 2])
            for qt in range(8):
                par = qt % 2
                psO = PS[3][:, par * 512:(par + 1) * 512]
                for kc in range(NKC):
                    w = []
                    if kc == 0:
                        w = [(sVL[(hq * 2 + dh) % 2], vld), (sVE, linvc),
                             (sSC, at_free.get(par, 0))]
                    E("tensor", lambda e, p=psO, kc=kc, qt=qt:
                      e.matmul(p, PT[:, kc, qt * P:(qt + 1) * P],
                               V48[:, kc, :], start=(kc == 0),
                               stop=(kc == NKC - 1),
                               skip_group_check=True),
                      waits=w, inc=(sTE if kc == NKC - 1 else None))
                grp = sTE.n
                ev = E("scalar", lambda e, p=psO, par=par, qt=qt:
                       e.activation(at_bf[par], p, AF.Identity,
                                    scale=linv_cols[:, qt:qt + 1]),
                       waits=[(sTE, grp)], inc=sSC, drain=False)
                at_free[par] = ev
                row0 = hq * 1024 + qt * P
                DMA("sync", at_h[row0:row0 + P, dh * 512:(dh + 1) * 512],
                    at_bf[par], waits=[(sSC, ev)], inc=sAT)
                wq_tick(1)
        if hq == 1:
            scores_te = scores_te_mark
        pv_te = sTE.n
    at_total = (sAT, sAT.n)
    d_te = sTE.n

    d_ve = sVE.n
    d_gc = sGC.n
    d_sc = sSC.n

    while wq_state["i"] < len(wq_srcs):
        wq_tick()
    wqs_tot = [(sWS[0], sWS[0].n), (sWS[1], sWS[1].n)]
    DMA("gpsimd", WG.rearrange("p c f -> p c f"),
        wgq_h.rearrange("(c p) f -> p c f", p=P),
        waits=wqs_tot + [(sTE, scores_te)], inc=sWG)
    DMA("gpsimd", WU.rearrange("p c f -> p c f"),
        wuq_h.rearrange("(c p) f -> p c f", p=P),
        waits=[(sTE, d_te)], inc=sWU)

    xld = None
    for c in range(CD):
        w = []
        if c == 0:
            w = [(sTE, d_te), (sXI, sXI.n), (sVE, d_ve), (sGC, d_gc)]
        xld = DMA("sync", XI12[:, c, 0:2048],
                  xi_h[0:TOWN, c * P:(c + 1) * P], waits=w, inc=sXT,
                  transpose=True)
    H = AWF[:, 8192:12288]
    gsil = S8F[:, 0:1024]
    hib = MSC8[:]
    gu_ev = {}
    hi_st_last = None
    hi_snap = {}
    dsc_done = {}
    for t in range(NOWN):
        for iq in range(4):
            gi = t * 4 + iq
            pg = PS[0] if iq % 2 == 0 else PS[2]
            pu = PS[1] if iq % 2 == 0 else PS[3]
            for c in range(CD):
                w = []
                if c == 0:
                    w = [(sXT, xld)]
                    if gi <= 1:
                        w.append((sSC, d_sc))
                    pe = gu_ev.get(gi - 2)
                    if pe:
                        w += [(sSC, pe[0]), (sVE, pe[1])]
                    w += [(sWG, 16), (sWU, 16)]
                lhsT = XI12[:, c, t * P:(t + 1) * P]
                E("tensor", lambda e, l=lhsT, p=pg, c=c, iq=iq:
                  e.matmul(p[:, 0:512], l,
                           WG[:, c, iq * 1024:iq * 1024 + 512],
                           start=(c == 0), stop=(c == CD - 1),
                           skip_group_check=True), waits=w)
                E("tensor", lambda e, l=lhsT, p=pg, c=c, iq=iq:
                  e.matmul(p[:, 512:1024], l,
                           WG[:, c, iq * 1024 + 512:(iq + 1) * 1024],
                           start=(c == 0), stop=(c == CD - 1),
                           skip_group_check=True))
                E("tensor", lambda e, l=lhsT, p=pu, c=c, iq=iq:
                  e.matmul(p[:, 0:512], l,
                           WU[:, c, iq * 1024:iq * 1024 + 512],
                           start=(c == 0), stop=(c == CD - 1),
                           skip_group_check=True))
                E("tensor", lambda e, l=lhsT, p=pu, c=c, iq=iq:
                  e.matmul(p[:, 512:1024], l,
                           WU[:, c, iq * 1024 + 512:(iq + 1) * 1024],
                           start=(c == 0), stop=(c == CD - 1),
                           skip_group_check=True),
                  inc=(sTE if c == CD - 1 else None))
            grp = sTE.n
            wsil = [(sTE, grp), (sVE, eprol)]
            if gi >= 1:
                wsil.append((sVE, gu_ev[gi - 1][1]))
            g_sc = E("scalar", lambda e, p=pg, t=t: e.activation(
                gsil, p[:], AF.Silu, scale=gscale_all[:, t:t + 1]),
                waits=wsil, inc=sSC, drain=False)
            h_vc = E("vector", lambda e, p=pu, iq=iq: e.tensor_tensor(
                H[:, iq * 1024:(iq + 1) * 1024], gsil, p[:], ALU.mult),
                waits=[(sSC, g_sc)], inc=sVE, drain=False)
            gu_ev[gi] = (g_sc, h_vc)
        E("vector", lambda e: e.reduce_max(
            hm_col[:], H, AX.X, apply_absolute_value=True))
        E("vector", lambda e, t=t: e.tensor_tensor(
            tc_col[:], hm_col[:], thr_all[:, t:t + 1], ALU.max))
        tcv = E("vector", lambda e: e.tensor_copy(
            gsil[:, 0:1], tc_col[:]), inc=sVE)
        E("scalar", lambda e: e.activation(
            gsil[:, 512:1024], gsil[:, 0:512], AF.Ln), waits=[(sVE, tcv)])
        hsx = E("scalar", lambda e: e.activation(
            gsil[:, 0:512], gsil[:, 512:1024], AF.Exp, scale=-1.0), inc=sSC)
        E("vector", lambda e: e.tensor_tensor(
            e1_col[:], tc_col[:], gsil[:, 0:1], ALU.mult), waits=[(sSC, hsx)])
        E("vector", lambda e: e.tensor_scalar(
            e1_col[:], e1_col[:], -1.0, 2.0, ALU.mult, ALU.add))
        E("vector", lambda e: e.tensor_tensor(
            e1_col[:], gsil[:, 0:1], e1_col[:], ALU.mult))
        E("vector", lambda e: e.tensor_scalar(
            hs_col[:], e1_col[:], 127.0, None, ALU.mult))
        E("vector", lambda e, t=t: e.tensor_tensor(
            d1_col[:], tc_col[:], isx_all[:, t:t + 1], ALU.mult))
        dsc_done[t] = E("vector", lambda e, t=t: e.tensor_tensor(
            dscale_all[:, t:t + 1], d1_col[:], cc_col[:], ALU.mult), inc=sVE)
        for s4 in range(4):
            w = ([hi_st_last] if hi_st_last else []) if s4 == 0 else []
            if s4 == 0 and t == 0:
                w.append(at_total)
                w += wqs_tot
            E("vector", lambda e, s4=s4: e.tensor_tensor(
                H[:, s4 * 1024:(s4 + 1) * 1024],
                H[:, s4 * 1024:(s4 + 1) * 1024],
                hs_col[:].broadcast_to((P, 1024)), ALU.mult), waits=w,
              drain=False)
            E("vector", lambda e, s4=s4: e.tensor_scalar(
                hib[:, s4 * 1024:(s4 + 1) * 1024],
                H[:, s4 * 1024:(s4 + 1) * 1024], RC, RC,
                ALU.add, ALU.subtract),
              inc=(sVE if s4 == 3 else None), drain=(s4 == 3))
        hv = sVE.n
        hi_st_last = (sHI[t % 4], DMA(
            "gpsimd", hi_h[t * P:(t + 1) * P, :], hib[:],
            waits=[(sVE, hv)], inc=sHI[t % 4]))
        hi_snap[t] = [(s, s.n) for s in sHI]
    ea1_te = sTE.n
    ea1_gp = sGP.n
    ea1_sc = sSC.n
    ea1_ve = sVE.n

    atb = [AWB[:, 0:1024], AWB[:, 1024:2048]]
    xb = [AWF[:, 1024:2048], AWF[:, 2048:3072]]
    otb = [AWF[:, 3072:4096], AWF[:, 4096:5120]]
    oub = [AWF[:, 5120:6144], AWF[:, 6144:7168]]
    wdb = [P8F[:, 0:1024], P8F[:, 1024:2048]]
    wd_ready = {}
    ld_free = {0: [], 1: []}
    ot_ev = {0: 0, 1: 0}
    ou_done = {0: 0, 1: 0}
    down_te_half = ea1_te
    out_total = 0
    for th in range(2):
        hld = None
        for ic in range(NKC):
            w = []
            if ic == 0:
                w = hi_snap[th * 8 + 7] + [(sTE, down_te_half)]
            hld = DMA("sync", HFM[:, ic, :],
                      hi_h[th * 1024:(th + 1) * 1024, ic * P:(ic + 1) * P],
                      waits=w, inc=sHF, transpose=True)
        if th == 0:
            for g in range(4):
                wld = list(wqs_tot)
                if g == 0:
                    wld.append((sTE, ea1_te))
                DMA("gpsimd",
                    WD.rearrange("p i d -> p i d")[:, g * 8:(g + 1) * 8, :],
                    wdq_h.rearrange("(i p) d -> p i d", p=P)[
                        :, g * 8:(g + 1) * 8, :],
                    waits=wld, inc=sWD[g])
                for ic in range(g * 8, (g + 1) * 8):
                    wd_ready[ic] = (sWD[g], 16)
        for qt in range(8):
            tile = th * 8 + qt
            par = qt % 2
            psD = PS[par]
            ald = DMA("gpsimd", atb[par],
                      at_h[tile * P:(tile + 1) * P, :],
                      waits=list(ld_free[par]) + [at_total],
                      inc=sEB[par])
            xldb = DMA("gpsimd", xb[par], x_h[tile * P:(tile + 1) * P, :],
                       waits=ld_free[par], inc=sEB[2 + par])
            for ic in range(NKC):
                w = []
                if ic == 0:
                    w = [(sHF, hld), (sSC, ot_ev[par])]
                    if th == 0 and qt < 2:
                        w += [(sSC, ea1_sc), (sVE, ea1_ve)]
                if th == 0:
                    w.append(wd_ready[ic])
                lhsT = HFM[:, ic, qt * P:(qt + 1) * P]
                E("tensor", lambda e, l=lhsT, p=psD, ic=ic:
                  e.matmul(p[:, 0:512], l, WD[:, ic, 0:512],
                           start=(ic == 0), stop=(ic == NKC - 1),
                           skip_group_check=True), waits=w)
                E("tensor", lambda e, l=lhsT, p=psD, ic=ic:
                  e.matmul(p[:, 512:1024], l, WD[:, ic, 512:1024],
                           start=(ic == 0), stop=(ic == NKC - 1),
                           skip_group_check=True),
                  inc=(sTE if ic == NKC - 1 else None))
            grp = sTE.n
            ot_ev[par] = E("scalar", lambda e, p=psD, par=par, tile=tile:
                           e.activation(otb[par], p[:], AF.Identity,
                                        scale=dscale_all[:, tile:tile + 1]),
                           waits=[(sTE, grp), (sVE, dsc_done[tile])],
                           inc=sSC, drain=False)
            E("vector", lambda e, par=par: e.tensor_add(
                otb[par], otb[par], atb[par]),
              waits=[(sSC, ot_ev[par]), (sEB[par], ald),
                     (sVE, ou_done[par])],
              drain=False)
            ov = E("vector", lambda e, par=par: e.tensor_add(
                oub[par], otb[par], xb[par]),
                waits=[(sEB[2 + par], xldb)], inc=sVE, drain=False)
            ou_done[par] = ov
            ost = DMA("sync", out_h[tile * P:(tile + 1) * P, :], oub[par],
                      waits=[(sVE, ov)], inc=sSY)
            ld_free[par] = [(sVE, ov)]
        down_te_half = sTE.n
    out_total = sSY.n

    E("sync", lambda e: e.nop(), waits=[(sSY, out_total)])
    E("gpsimd", lambda e: e.nop(), waits=[(sGP, sGP.n)])

    for s in bd.sems:
        s.handle = nc.alloc_semaphore(name=s.name)
    with nc.Block() as block:
        def mk(name):
            def run(eng):
                for fn, waits, inc, step in bd.streams[name]:
                    for (s, v) in waits:
                        eng.wait_ge(s.handle, v)
                    ins = fn(eng)
                    if inc is not None:
                        ins.then_inc(inc.handle, step)
            return run
        block.sync(mk("sync"))
        block.gpsimd(mk("gpsimd"))
        block.tensor(mk("tensor"))
        block.vector(mk("vector"))
        block.scalar(mk("scalar"))
    return nc


_NC_CACHE = None


def kernel(L0_attn_x, L0_attn_Wq, L0_attn_bq, L0_attn_Wk, L0_attn_bk,
           L0_attn_Wv, L0_attn_bv, L0_ffn_w_gate, L0_ffn_w_up,
           L0_ffn_w_down):
    global _NC_CACHE
    x = np.ascontiguousarray(np.asarray(L0_attn_x, dtype=np.float32))
    wq = np.asarray(L0_attn_Wq, dtype=np.float32)
    wk = np.asarray(L0_attn_Wk, dtype=np.float32)
    wv = np.asarray(L0_attn_Wv, dtype=np.float32)
    wg = np.asarray(L0_ffn_w_gate, dtype=np.float32)
    wu = np.asarray(L0_ffn_w_up, dtype=np.float32)
    wd = np.asarray(L0_ffn_w_down, dtype=np.float32)
    shared = {
        "wqT": np.ascontiguousarray(wq.T),
        "wkT": np.ascontiguousarray(wk.T),
        "wvT": np.ascontiguousarray(wv.T),
        "bq": np.ascontiguousarray(np.asarray(L0_attn_bq, dtype=np.float32)),
        "bk": np.ascontiguousarray(np.asarray(L0_attn_bk, dtype=np.float32)),
        "bv": np.ascontiguousarray(np.asarray(L0_attn_bv, dtype=np.float32)),
        "wgT": np.ascontiguousarray(wg.T),
        "wuT": np.ascontiguousarray(wu.T),
        "wdT": np.ascontiguousarray(wd.T),
    }
    in_maps = []
    for core in range(8):
        b, h = core // 2, core % 2
        xb = x[b]
        xcore = np.concatenate([xb[h * TOWN:(h + 1) * TOWN],
                                xb[(1 - h) * TOWN:(2 - h) * TOWN]], axis=0)
        m = {"x": np.ascontiguousarray(xcore)}
        m.update(shared)
        in_maps.append(m)

    if _NC_CACHE is None:
        _NC_CACHE = build_core()
    nc = _NC_CACHE
    res = run_bass_kernel_spmd(nc, in_maps, core_ids=list(range(8)))
    out = np.empty((4, T, D), dtype=np.float32)
    for core in range(8):
        b, h = core // 2, core % 2
        out[b, h * TOWN:(h + 1) * TOWN] = res.results[core]["out"]
    return out
